# version 11 (final)
# baseline (speedup 1.0000x reference)
import contextlib
import numpy as np
import concourse.bass as bass
import concourse.mybir as mybir
from concourse.bass_utils import run_bass_kernel_spmd

F32 = mybir.dt.float32
F16 = mybir.dt.float16
BF16 = mybir.dt.bfloat16
I32 = mybir.dt.int32
AF = mybir.ActivationFunctionType
OP = mybir.AluOpType
AX = mybir.AxisListType

D = 1024
T = 4096
H = 4096
NCH = 8
HCH = 32
TT = 512
NT = T // TT
TA = 256
NA = T // TA
CMAGIC = 12582912.0
EPS = 1e-5
QP = 127.0
MINR = 1e-6


class Sem:
    def __init__(self, stack, nc, name):
        self.h = stack.enter_context(nc.semaphore(name))
        self.n = 0

    def inc(self, inst, k=1):
        inst.then_inc(self.h, k)
        self.n += k
        return self.n


class Ring:

    def __init__(self, stack, nc, name, depth):
        self.sems = [Sem(stack, nc, f"{name}{k}") for k in range(depth)]
        self.depth = depth
        self.k = 0

    def dma(self, eng, out, in_):
        d, k = self.depth, self.k
        s = self.sems[k % d]
        if k >= d:
            eng.wait_ge(s.h, 16 * (k // d))
        s.inc(eng.dma_start(out=out, in_=in_), 16)
        self.k += 1
        return k

    def wv(self, k):
        d = self.depth
        return self.sems[k % d].h, 16 * (k // d + 1)

    def wait(self, eng, k):
        h, v = self.wv(k)
        eng.wait_ge(h, v)

    def wait_all(self, eng):
        for s in self.sems:
            if s.n:
                eng.wait_ge(s.h, s.n)


def _flat(th, off, n, rowlen):
    return bass.AP(th, off, [[rowlen, 128], [1, n]])


def build_program():
    nc = bass.Bass("TRN2", target_bir_lowering=False, debug=False)

    xT = nc.dram_tensor("xT", [D, T], F32, kind="ExternalInput").ap()
    w1T = nc.dram_tensor("w1T", [D, H], F32, kind="ExternalInput").ap()
    w2T = nc.dram_tensor("w2T", [H, D], F32, kind="ExternalInput").ap()
    wsh = nc.dram_tensor("wsh", [128, 8192], F32, kind="ExternalInput").ap()
    winT = nc.dram_tensor("winT", [D, D], F32, kind="ExternalInput").ap()
    waT = nc.dram_tensor("waT", [D, D], F32, kind="ExternalInput").ap()
    wbT = nc.dram_tensor("wbT", [D, D], F32, kind="ExternalInput").ap()
    wvT = nc.dram_tensor("wvT", [D, D], F32, kind="ExternalInput").ap()
    woT = nc.dram_tensor("woT", [D, D], F32, kind="ExternalInput").ap()
    b1p = nc.dram_tensor("b1p", [128, HCH], F32, kind="ExternalInput").ap()
    b2p = nc.dram_tensor("b2p", [128, NCH], F32, kind="ExternalInput").ap()
    bap = nc.dram_tensor("bap", [128, NCH], F32, kind="ExternalInput").ap()
    bbp = nc.dram_tensor("bbp", [128, NCH], F32, kind="ExternalInput").ap()
    bvp = nc.dram_tensor("bvp", [128, NCH], F32, kind="ExternalInput").ap()
    outT = nc.dram_tensor("outT", [D, T], F32, kind="ExternalOutput").ap()

    ln_d = nc.dram_tensor("ln_d", [NCH, 128, T], F16, kind="Internal").ap()
    x1_d = nc.dram_tensor("x1_d", [NCH, 128, T], F16, kind="Internal").ap()
    x2_d = nc.dram_tensor("x2_d", [NCH, 128, T], F16, kind="Internal").ap()
    h_d = nc.dram_tensor("h_d", [NT, HCH, 128, TT], F16, kind="Internal").ap()
    w1q_t = nc.dram_tensor("w1q_d", [HCH, 128, NCH * 128], BF16,
                           kind="Internal")
    w1q_d = w1q_t.ap()
    w2q_t = nc.dram_tensor("w2q_d", [NCH, 128, HCH, 128], BF16,
                           kind="Internal")
    w2q_d = w2q_t.ap()

    def w1qd_ap(j):
        return bass.AP(w1q_t, j * 128 * NCH * 128,
                       [[NCH * 128, 128], [128, NCH], [1, 128]])
    sbounce_t = nc.dram_tensor("sbounce", [1, 16], F32, kind="Internal")
    sbounce = sbounce_t.ap()
    watt_t = nc.dram_tensor("watt_d", [5, 128, 8192], BF16,
                            kind="Internal")
    watt_d = watt_t.ap()
    cc_in = nc.dram_tensor("cc_in", [1, 128], F32, kind="Internal").ap()
    cc_out = nc.dram_tensor("cc_out", [1, 128], F32, kind="Internal").ap()
    ccw_in = nc.dram_tensor("ccw_in", [1, 256], F32, kind="Internal").ap()
    ccw_out = nc.dram_tensor("ccw_out", [1, 256], F32, kind="Internal").ap()

    xT_r = xT.rearrange("(c p) t -> p c t", p=128)
    w1T_r = w1T.rearrange("(c p) j -> p c j", p=128)
    w2T_r = w2T.rearrange("(a p) d -> p a d", p=128)
    outT_r = outT.rearrange("(c p) t -> p c t", p=128)
    winT_r = winT.rearrange("(c p) e -> p c e", p=128)
    waT_r = waT.rearrange("(c p) e -> p c e", p=128)
    wbT_r = wbT.rearrange("(c p) e -> p c e", p=128)
    wvT_r = wvT.rearrange("(c p) e -> p c e", p=128)
    woT_r = woT.rearrange("(c p) e -> p c e", p=128)

    st = contextlib.ExitStack()

    def sb(name, shape, dt):
        return st.enter_context(nc.sbuf_tensor(name, shape, dt))

    ones_bf = sb("ones_bf", [128, 128], BF16)
    cst = sb("cst", [128, 4], F32)
    EPS_A = cst[:, 0:1]
    CP_A = cst[:, 1:2]
    CN_A = cst[:, 2:3]
    b1sb = sb("b1sb", [128, HCH], F32)
    b2sb = sb("b2sb", [128, NCH], F32)
    b1s = sb("b1s", [128, HCH], F32)
    b2s = sb("b2s", [128, NCH], F32)
    bap_s = sb("bap_s", [128, NCH], F32)
    bbp_s = sb("bbp_s", [128, NCH], F32)
    bvp_s = sb("bvp_s", [128, NCH], F32)
    sw1_b = sb("sw1_b", [128, 1], F32)
    sw2_b = sb("sw2_b", [128, 1], F32)
    invw1 = sb("invw1", [128, 1], F32)
    invw2 = sb("invw2", [128, 1], F32)
    scl = sb("scl", [128, 12], F32)
    s11 = sb("s11", [1, 2], F32)
    row128 = sb("row128", [1, 128], F32)
    row256 = sb("row256", [1, 256], F32)
    rmax = sb("rmax", [128, 1], F32)
    tmax = sb("tmax", [128, 1], F32)

    pb = [st.enter_context(nc.psum_tensor(f"pb{i}", [128, 512], F32))
          for i in range(8)]

    S = {}
    for name in ["ld", "ld2", "sp", "g", "cc", "vch", "wact", "w2act",
                 "sq", "mm2", "sd", "mm", "ev",
                 "sq2", "mm3", "sd3",
                 "mi", "xps", "ale", "bee", "vee", "pv", "sg", "mo"]:
        S[name] = Sem(st, nc, "s_" + name)

    VCH = S["vch"]

    def C(ins):
        if VCH.n > 0:
            ins.wait_op(VCH.h, VCH.n, "sem-ge")
        VCH.inc(ins)
        return VCH.n

    def snap():
        return {k: s.n for k, s in S.items()}

    def DMA(eng, sem, out, in_):
        if sem.n > 0:
            eng.wait_ge(sem.h, sem.n)
        return sem.inc(eng.dma_start(out=out, in_=in_), 16)

    R_x = Ring(st, nc, "r_x", 4)
    R_w1f = Ring(st, nc, "r_w1f", 2)
    R_lnsp = Ring(st, nc, "r_lnsp", 2)
    R_lb = Ring(st, nc, "r_lb", 4)
    R_w2f = Ring(st, nc, "r_w2f", 2)
    R_hsp = Ring(st, nc, "r_hsp", 4)
    R_w1qsp = Ring(st, nc, "r_w1qsp", 2)
    R_w2qsp = Ring(st, nc, "r_w2qsp", 4)
    R_h = Ring(st, nc, "r_h", 4)
    R_w2b = Ring(st, nc, "r_w2b", 2)
    R_x0 = Ring(st, nc, "r_x0", 2)
    R_x1sp = Ring(st, nc, "r_x1sp", 2)
    R_xa = Ring(st, nc, "r_xa", 4)
    R_x2sp = Ring(st, nc, "r_x2sp", 2)
    R_ln3sp = Ring(st, nc, "r_ln3sp", 2)
    R_w1ql = Ring(st, nc, "r_w1ql", 4)
    R_wfl = Ring(st, nc, "r_wfl", 2)
    R_wfsp = Ring(st, nc, "r_wfsp", 2)
    R_wal = Ring(st, nc, "r_wal", 2)
    RINGS = [R_x, R_w1f, R_lnsp, R_lb, R_w2f, R_hsp, R_w1qsp, R_w2qsp,
             R_h, R_w2b, R_x0, R_x1sp, R_xa, R_x2sp, R_ln3sp, R_w1ql,
             R_wfl, R_wfsp, R_wal]

    def take_guard():
        g = []
        for s in S.values():
            if s.n > 0:
                g.append((s.h, s.n))
        for r in RINGS:
            for s in r.sems:
                if s.n > 0:
                    g.append((s.h, s.n))
        return g

    def barrier(eng, g):
        for h, v in g:
            eng.wait_ge(h, v)

    inv_n = {}

    def fin_scalar_v(v, gwait, src):
        v.wait_ge(S["g"].h, gwait)
        C(v.reduce_max(out=s11[0:1, 0:1], in_=src, axis=AX.X))
        C(v.tensor_scalar(s11[0:1, 0:1], s11[0:1, 0:1], MINR, None, OP.max))
        return C(v.tensor_scalar(s11[0:1, 0:1], s11[0:1, 0:1], 1.0 / QP,
                                 None, OP.mult))

    def fin_bcast_g(g, n_s, scol, dst):
        g.wait_ge(VCH.h, n_s)
        DMA(g, S["g"], sbounce[0:1, scol:scol + 1], s11[0:1, 0:1])
        DMA(g, S["g"], dst, bass.AP(sbounce_t, scol, [[0, 128], [1, 1]]))

    with nc.Block("b0") as blk:
        @blk.vector
        def _(v):
            C(v.memset(ones_bf[:], 1.0))
            C(v.memset(cst[:, 0:1], EPS))
            C(v.memset(cst[:, 1:2], CMAGIC))
            C(v.memset(cst[:, 2:3], -CMAGIC))
            C(v.memset(rmax[:], 0.0))

        @blk.sync
        def _(s):
            DMA(s, S["ld"], b1sb[:], b1p)
            DMA(s, S["ld"], b2sb[:], b2p)
            DMA(s, S["ld"], bap_s[:], bap)
            DMA(s, S["ld"], bbp_s[:], bbp)
            DMA(s, S["ld"], bvp_s[:], bvp)
            s.wait_ge(S["ld"].h, S["ld"].n)

    def emit_b12(w1q_sb):
        b = snap()
        with (
            nc.sbuf_tensor("a_x", [128, 3, NCH, TT], F32) as a_x,
            nc.sbuf_tensor("a_xbf", [128, 1, NCH, TT], BF16) as a_xbf,
            nc.sbuf_tensor("a_sqbf", [128, 2, NCH, TT], BF16) as a_sqbf,
            nc.sbuf_tensor("a_lnh", [128, 2, NCH, TT], F16) as a_lnh,
            nc.sbuf_tensor("a_st", [128, 1, 4, TT], F32) as a_st,
            nc.sbuf_tensor("a_rstd", [128, 1, TT], F32) as a_rstd,
            nc.sbuf_tensor("wshb", [128, 2, 2048], F32) as wshb,
            nc.sbuf_tensor("wstage", [128, 2, 1024], F32) as wstage,
            nc.sbuf_tensor("wqi1", [128, 4, 1024], I32) as wqi1,
            nc.sbuf_tensor("rmw1", [128, 1], F32) as rmw1,
            nc.sbuf_tensor("rmw2", [128, 1], F32) as rmw2,
            nc.Block("b12") as blk,
        ):
            NROW = 3 * NCH * TT
            XBROW = NCH * TT
            SROW = 4 * TT
            RROW = TT
            WSROW = 2 * 2048
            WROW = 2 * 1024
            QROW = 4 * 1024
            VC = {}

            def xf(i):
                return _flat(a_x, (i % 3) * NCH * TT, NCH * TT, NROW)

            def x3(i):
                return bass.AP(a_x, (i % 3) * NCH * TT,
                               [[NROW, 128], [TT, NCH], [1, TT]])

            def stt(k, slot):
                return _flat(a_st, slot * TT, TT, SROW)

            def stt_b(k, slot):
                return bass.AP(a_st, slot * TT,
                               [[SROW, 128], [0, NCH], [1, TT]])

            def rstd_b(k):
                return bass.AP(a_rstd, 0, [[RROW, 128], [0, NCH],
                                           [1, TT]])

            def wshl(item):
                return _flat(wshb, (item % 2) * 2048, 2048, WSROW)

            def wsl(c):
                return _flat(wstage, (c % 2) * 1024, 1024, WROW)

            def wqs(c):
                return _flat(wqi1, (c % 4) * 1024, 1024, QROW)

            @blk.vector
            def _(v):
                R_w1f.wait(v, 0)
                VC["wm", 0] = C(v.reduce_max(out=rmw1[:], in_=wshl(0),
                                             axis=AX.X,
                                             apply_absolute_value=True))
                R_w1f.wait(v, 1)
                C(v.reduce_max(out=tmax[:], in_=wshl(1), axis=AX.X,
                               apply_absolute_value=True))
                VC["wm", 1] = C(v.tensor_tensor(out=rmw1[:], in0=rmw1[:],
                                                in1=tmax[:], op=OP.max))
                R_w1f.wait(v, 2)
                VC["wm", 2] = C(v.reduce_max(out=rmw2[:], in_=wshl(2),
                                             axis=AX.X,
                                             apply_absolute_value=True))
                R_w1f.wait(v, 3)
                C(v.reduce_max(out=tmax[:], in_=wshl(3), axis=AX.X,
                               apply_absolute_value=True))
                VC["wm", 3] = C(v.tensor_tensor(out=rmw2[:], in0=rmw2[:],
                                                in1=tmax[:], op=OP.max))

                def XBF(i):
                    R_x.wait(v, i)
                    if i >= 1:
                        v.wait_ge(S["mm2"].h, b["mm2"] + i)
                    VC["xbf", i] = C(v.tensor_copy(
                        out=_flat(a_xbf, 0, NCH * TT, XBROW),
                        in_=xf(i)))

                def TAIL(i):
                    k = i % 2
                    pa, pq = pb[2 * k], pb[2 * k + 1]
                    v.wait_ge(S["mm2"].h, b["mm2"] + i + 1)

                    C(v.tensor_scalar(stt(k, 0), pa[:, 0:TT], 1.0 / D, None,
                                      OP.mult))
                    C(v.tensor_scalar(stt(k, 1), pq[:, 0:TT], 1.0 / D, None,
                                      OP.mult))
                    C(v.tensor_mul(stt(k, 3), stt(k, 0), stt(k, 0)))
                    VC["var", i] = C(v.tensor_sub(stt(k, 2), stt(k, 1),
                                                  stt(k, 3)))
                    v.wait_ge(S["sd"].h, b["sd"] + i + 1)
                    C(v.reciprocal(out=_flat(a_rstd, 0, TT, RROW),
                                   in_=stt(k, 3)))
                    if i >= 2:
                        R_lnsp.wait(v, i - 2)
                    lt = bass.AP(a_lnh, k * NCH * TT,
                                 [[2 * NCH * TT, 128], [TT, NCH], [1, TT]])
                    C(v.tensor_tensor(out=lt, in0=x3(i), in1=stt_b(k, 0),
                                      op=OP.subtract))
                    C(v.tensor_tensor(out=lt, in0=lt, in1=rstd_b(k),
                                      op=OP.mult))
                    C(v.reduce_max(out=tmax[:],
                                   in_=_flat(a_lnh, k * NCH * TT, NCH * TT,
                                             2 * NCH * TT),
                                   axis=AX.X, apply_absolute_value=True))
                    VC["ln", i] = C(v.tensor_tensor(out=rmax[:], in0=rmax[:],
                                                    in1=tmax[:], op=OP.max))

                def WQ(c):
                    v.wait_ge(S["wact"].h, b["wact"] + 2 * (c + 1))
                    fc, q = c // 4, c % 4
                    VC["wq", c] = C(v.tensor_copy(
                        out=_flat(w1q_sb, fc * H + q * 1024, 1024, NCH * H),
                        in_=wqs(c)))

                wq_done = 0
                for i in range(NT):
                    XBF(i)
                    if i >= 1:
                        TAIL(i - 1)
                    if i == 2:
                        n1 = fin_scalar_v(v, b["g"] + 48,
                                          row256[0:1, 0:128])
                        VC["s1w"] = n1
                        v.wait_ge(S["g"].h, b["g"] + 80)
                        inv_n["w1"] = C(v.reciprocal(out=invw1[:],
                                                     in_=sw1_b[:]))
                        n2 = fin_scalar_v(v, b["g"] + 48,
                                          row256[0:1, 128:256])
                        VC["s2w"] = n2
                        v.wait_ge(S["g"].h, b["g"] + 112)
                        inv_n["w2"] = C(v.reciprocal(out=invw2[:],
                                                     in_=sw2_b[:]))
                    if i >= 3:
                        for _ in range(4):
                            WQ(wq_done)
                            wq_done += 1
                TAIL(NT - 1)
                while wq_done < 32:
                    WQ(wq_done)
                    wq_done += 1
                VC["wq_last"] = VC["wq", 31]

            @blk.sync
            def _(s):
                nsp = 0
                for i in range(NT):
                    if i >= 3:
                        s.wait_ge(VCH.h, VC["ln", i - 3])
                        s.wait_ge(S["sq"].h, b["sq"] + i - 2)
                    R_x.dma(s, x3(i), xT_r[:, :, i * TT:(i + 1) * TT])
                    if i >= 3:
                        j = i - 3
                        s.wait_ge(VCH.h, VC["ln", j])
                        dst = ln_d[:, :, j * TT:(j + 1) * TT].transpose(
                            [1, 0, 2])
                        R_lnsp.dma(s, dst,
                                   bass.AP(a_lnh, (j % 2) * NCH * TT,
                                           [[2 * NCH * TT, 128], [TT, NCH],
                                            [1, TT]]))
                        nsp += 1
                for j in range(nsp, NT):
                    s.wait_ge(VCH.h, VC["ln", j])
                    dst = ln_d[:, :, j * TT:(j + 1) * TT].transpose([1, 0, 2])
                    R_lnsp.dma(s, dst,
                               bass.AP(a_lnh, (j % 2) * NCH * TT,
                                       [[2 * NCH * TT, 128], [TT, NCH],
                                        [1, TT]]))

            @blk.scalar
            def _(sc):
                def SQ(i):
                    R_x.wait(sc, i)
                    if i >= 2:
                        sc.wait_ge(S["mm2"].h, b["mm2"] + i - 1)
                    S["sq"].inc(sc.activation(
                        out=_flat(a_sqbf, (i % 2) * NCH * TT, NCH * TT,
                                  2 * NCH * TT),
                        in_=xf(i), func=AF.Square))

                def SD(i):
                    sc.wait_ge(VCH.h, VC["var", i])
                    S["sd"].inc(sc.activation(
                        out=stt(i % 2, 3), in_=stt(i % 2, 2),
                        func=AF.Sqrt, bias=EPS_A))

                def WQA1(c):
                    R_w1f.wait(sc, 4 + c)
                    sc.wait_ge(VCH.h, inv_n["w1"])
                    S["wact"].inc(sc.activation(
                        out=wsl(c), in_=wsl(c), func=AF.Identity,
                        scale=invw1[:, 0:1], bias=CP_A))
                    if c >= 4:
                        sc.wait_ge(VCH.h, VC["wq", c - 4])
                    i2 = sc.activation(out=wqs(c), in_=wsl(c),
                                       func=AF.Identity, bias=CN_A)
                    i2.wait_op(S["wact"].h, S["wact"].n, "sem-ge")
                    S["wact"].inc(i2)

                wq_a = 0
                for i in range(NT):
                    SQ(i)
                    if i >= 1:
                        SD(i - 1)
                    if i >= 2:
                        for _ in range(4):
                            WQA1(wq_a)
                            wq_a += 1
                SD(NT - 1)
                while wq_a < 32:
                    WQA1(wq_a)
                    wq_a += 1

            @blk.tensor
            def _(t):
                for i in range(NT):
                    t.wait_ge(VCH.h, VC["xbf", i])
                    t.wait_ge(S["sq"].h, b["sq"] + i + 1)
                    if i >= 2:
                        t.wait_ge(VCH.h, VC["var", i - 2])
                    pa, pq = pb[2 * (i % 2)], pb[2 * (i % 2) + 1]
                    for c in range(NCH):
                        t.matmul(pa[:, 0:TT], ones_bf[:],
                                 _flat(a_xbf, c * TT, TT, XBROW),
                                 start=(c == 0), stop=(c == NCH - 1))
                    for c in range(NCH):
                        ins = t.matmul(pq[:, 0:TT], ones_bf[:],
                                       _flat(a_sqbf, (i % 2) * NCH * TT
                                             + c * TT, TT, 2 * NCH * TT),
                                       start=(c == 0), stop=(c == NCH - 1))
                    S["mm2"].inc(ins)

            @blk.gpsimd
            def _(g):
                g.wait_ge(S["ld"].h, S["ld"].n)
                for q in range(4):
                    if q >= 2:
                        g.wait_ge(VCH.h, VC["wm", q - 2])
                    R_w1f.dma(g, wshl(q), wsh[:, q * 2048:(q + 1) * 2048])
                g.wait_ge(VCH.h, VC["wm", 3])
                DMA(g, S["g"], ccw_in[0:1, 0:128].transpose([1, 0]),
                    rmw1[:, 0:1])
                DMA(g, S["g"], ccw_in[0:1, 128:256].transpose([1, 0]),
                    rmw2[:, 0:1])
                g.wait_ge(S["g"].h, b["g"] + 32)
                S["cc"].inc(g.collective_compute(
                    "AllReduce", OP.max, replica_groups=[list(range(8))],
                    ins=[ccw_in], outs=[ccw_out]))
                g.wait_ge(S["cc"].h, 1)
                DMA(g, S["g"], row256[:], ccw_out)
                fin_bcast_g(g, VC["s1w"], 0, sw1_b[:])
                fin_bcast_g(g, VC["s2w"], 1, sw2_b[:])
                for c in range(24):
                    if c >= 2:
                        g.wait_ge(S["wact"].h, b["wact"] + 2 * (c - 1))
                    fc, q = c // 4, c % 4
                    R_w1f.dma(g, wsl(c),
                              w1T_r[:, fc, q * 1024:q * 1024 + 1024])
                g.wait_ge(VCH.h, VC["ln", NT - 1])
                DMA(g, S["sp"], cc_in[0:1, :].transpose([1, 0]), rmax[:, 0:1])
                g.wait_ge(S["sp"].h, S["sp"].n)
                S["cc"].inc(g.collective_compute(
                    "AllReduce", OP.max, replica_groups=[list(range(8))],
                    ins=[cc_in], outs=[cc_out]))
                for c in range(24, 32):
                    g.wait_ge(S["wact"].h, b["wact"] + 2 * (c - 1))
                    fc, q = c // 4, c % 4
                    R_w1f.dma(g, wsl(c),
                              w1T_r[:, fc, q * 1024:q * 1024 + 1024])
        return VC["wq_last"]

    def emit_passB(ncc, col, w1q_ap, first, guard, ln_wait_fn,
                   w1q_vc, w1q_sb=None):
        b = snap()
        sfx = "_%d" % ncc
        lb_base = R_lb.k
        hsp_base = R_hsp.k
        es = contextlib.ExitStack()
        lbuf = es.enter_context(
            nc.sbuf_tensor("lbuf" + sfx, [128, 2, NCH, TT], F16))
        qf = es.enter_context(
            nc.sbuf_tensor("qf" + sfx, [128, NCH, TT], F32))
        qib = es.enter_context(
            nc.sbuf_tensor("qib" + sfx, [128, NCH, TT], I32))
        qbf = es.enter_context(
            nc.sbuf_tensor("qbf" + sfx, [128, 2, NCH, TT], BF16))
        gbuf = es.enter_context(
            nc.sbuf_tensor("gbuf" + sfx, [128, 2, 4, TT], F16))
        if first:
            wfstg = es.enter_context(
                nc.sbuf_tensor("wfstg" + sfx, [128, 2, 4096], F32))
            wfbf = es.enter_context(
                nc.sbuf_tensor("wfbf" + sfx, [128, 2, 4096], BF16))
            w2st = es.enter_context(
                nc.sbuf_tensor("w2st" + sfx, [128, 2, 1024], F32))
            wqi2 = es.enter_context(
                nc.sbuf_tensor("wqi2" + sfx, [128, 2, 1024], I32))
            w2qb = es.enter_context(
                nc.sbuf_tensor("w2qb" + sfx, [128, 2, 1024], BF16))
        with nc.Block("b_ffnB" + str(ncc)) as blk:
            LROW = 2 * NCH * TT
            GROW = 2 * 4 * TT
            VC = {}

            @blk.vector
            def _(v):
                n_s = fin_scalar_v(v, b["g"] + 16, row128[:])
                VC["s"] = n_s
                v.wait_ge(S["g"].h, b["g"] + 48)
                C(v.reciprocal(out=scl[:, col + 1:col + 2],
                               in_=scl[:, col:col + 1]))
                C(v.tensor_mul(scl[:, col + 2:col + 3], sw1_b[:, 0:1],
                               scl[:, col:col + 1]))
                VC["fin"] = C(v.tensor_scalar(b1s[:], b1sb[:],
                                              scl[:, col + 2:col + 3], None,
                                              OP.mult))

                def QNT(i):
                    R_lb.wait(v, lb_base + i)
                    if i >= 2:
                        v.wait_ge(S["mm"].h, b["mm"] + HCH * (i - 1))
                    lf = _flat(lbuf, (i % 2) * NCH * TT, NCH * TT, LROW)
                    C(v.tensor_scalar(_flat(qf, 0, NCH * TT, NCH * TT), lf,
                                      scl[:, col + 1:col + 2], CMAGIC,
                                      OP.mult, OP.add))
                    C(v.tensor_scalar(_flat(qib, 0, NCH * TT, NCH * TT),
                                      _flat(qf, 0, NCH * TT, NCH * TT),
                                      CMAGIC, None, OP.subtract))
                    VC["qa", i] = C(v.tensor_copy(
                        out=_flat(qbf, (i % 2) * NCH * TT, NCH * TT, LROW),
                        in_=_flat(qib, 0, NCH * TT, NCH * TT)))

                def HMAX(gidx):
                    v.wait_ge(S["ev"].h, b["ev"] + 4 * (gidx + 1))
                    C(v.reduce_max(out=tmax[:],
                                   in_=_flat(gbuf, (gidx % 2) * 4 * TT,
                                             4 * TT, GROW),
                                   axis=AX.X, apply_absolute_value=True))
                    VC["hm", gidx] = C(v.tensor_tensor(
                        out=rmax[:], in0=rmax[:], in1=tmax[:], op=OP.max))

                def W2CP(a):
                    v.wait_ge(S["w2act"].h, b["w2act"] + 2 * (a + 1))
                    if a >= 2:
                        R_w2qsp.wait(v, a - 2)
                    VC["w2q", a] = C(v.tensor_copy(
                        out=_flat(w2qb, (a % 2) * 1024, 1024, 2048),
                        in_=_flat(wqi2, (a % 2) * 1024, 1024, 2048)))

                QNT(0)
                if NT > 1:
                    QNT(1)
                C(v.memset(rmax[:], 0.0))
                w2done = 0
                for gidx in range(8 * NT):
                    HMAX(gidx)
                    if (first and w2done < HCH and gidx >= 3
                            and gidx % 2 == 1):
                        W2CP(w2done)
                        w2done += 1
                    if gidx % 8 == 7 and gidx // 8 + 2 < NT:
                        QNT(gidx // 8 + 2)
                while first and w2done < HCH:
                    W2CP(w2done)
                    w2done += 1
                if first:
                    for t_ in range(10):
                        R_wfl.wait(v, t_)
                        if t_ >= 2:
                            R_wfsp.wait(v, t_ - 2)
                        VC["wcv", t_] = C(v.tensor_copy(
                            out=_flat(wfbf, (t_ % 2) * 4096, 4096, 2 * 4096),
                            in_=_flat(wfstg, (t_ % 2) * 4096, 4096,
                                      2 * 4096)))

            @blk.sync
            def _(s):
                barrier(s, guard)

                def LB(i):
                    if i >= 2:
                        s.wait_ge(VCH.h, VC["qa", i - 2])
                    ln_wait_fn(s, i)
                    src2 = ln_d[:, :, i * TT:(i + 1) * TT].transpose(
                        [1, 0, 2])
                    R_lb.dma(s,
                             bass.AP(lbuf, (i % 2) * NCH * TT,
                                     [[LROW, 128], [TT, NCH], [1, TT]]),
                             src2)

                LB(0)
                if NT > 1:
                    LB(1)
                for i in range(NT):
                    if first:
                        for a in range(4 * i, 4 * i + 4):
                            if a >= 2:
                                s.wait_ge(S["w2act"].h,
                                          b["w2act"] + 2 * (a - 1))
                            R_w2f.dma(s, _flat(w2st, (a % 2) * 1024, 1024,
                                               2048),
                                      w2T_r[:, a, :])
                    if i + 2 < NT:
                        LB(i + 2)

            @blk.tensor
            def _(t):
                t.wait_ge(VCH.h, VC["fin"])
                if w1q_vc:
                    t.wait_ge(VCH.h, w1q_vc)
                for i in range(NT):
                    t.wait_ge(VCH.h, VC["qa", i])
                    for j in range(HCH):
                        m = HCH * i + j
                        if m >= 2:
                            t.wait_ge(S["ev"].h, b["ev"] + m - 1)
                        if not first:
                            R_w1ql.wait(t, j)
                        pj = pb[j % 2]
                        for c in range(NCH):
                            ins = t.matmul(
                                pj[:, 0:TT],
                                w1q_ap(c, j),
                                _flat(qbf, (i % 2) * NCH * TT + c * TT, TT,
                                      LROW),
                                start=(c == 0), stop=(c == NCH - 1))
                        S["mm"].inc(ins)

            @blk.scalar
            def _(sc):
                def W2QA(a):
                    R_w2f.wait(sc, a)
                    sc.wait_ge(VCH.h, inv_n["w2"])
                    S["w2act"].inc(sc.activation(
                        out=_flat(w2st, (a % 2) * 1024, 1024, 2048),
                        in_=_flat(w2st, (a % 2) * 1024, 1024, 2048),
                        func=AF.Identity, scale=invw2[:, 0:1], bias=CP_A))
                    if a >= 2:
                        sc.wait_ge(VCH.h, VC["w2q", a - 2])
                    i2 = sc.activation(
                        out=_flat(wqi2, (a % 2) * 1024, 1024, 2048),
                        in_=_flat(w2st, (a % 2) * 1024, 1024, 2048),
                        func=AF.Identity, bias=CN_A)
                    i2.wait_op(S["w2act"].h, S["w2act"].n, "sem-ge")
                    S["w2act"].inc(i2)

                w2a = 0
                for i in range(NT):
                    for j in range(HCH):
                        m = HCH * i + j
                        sc.wait_ge(S["mm"].h, b["mm"] + m + 1)
                        gidx = m // 4
                        if m % 4 == 0 and gidx >= 2:
                            R_hsp.wait(sc, hsp_base + gidx - 2)
                            sc.wait_ge(VCH.h, VC["hm", gidx - 2])
                        S["ev"].inc(sc.activation(
                            out=_flat(gbuf, (gidx % 2) * 4 * TT
                                      + (m % 4) * TT, TT, GROW),
                            in_=pb[j % 2][:, 0:TT], func=AF.Gelu,
                            scale=scl[:, col + 2:col + 3],
                            bias=b1s[:, j:j + 1]))
                        if first and j % 8 == 7 and w2a < HCH:
                            W2QA(w2a)
                            w2a += 1
                while first and w2a < HCH:
                    W2QA(w2a)
                    w2a += 1

            @blk.gpsimd
            def _(g):
                g.wait_ge(S["cc"].h, ncc)
                DMA(g, S["g"], row128[:], cc_out[0:1, :])
                fin_bcast_g(g, VC["s"], 3 + ncc, scl[:, col:col + 1])
                if first:
                    g.wait_ge(VCH.h, w1q_vc)
                    for j in range(HCH):
                        src = bass.AP(w1q_sb, j * 128,
                                      [[NCH * H, 128], [H, NCH], [1, 128]])
                        R_w1qsp.dma(g, w1qd_ap(j), src)
                w2sp = 0
                for gidx in range(8 * NT):
                    i, gg = gidx // 8, gidx % 8
                    g.wait_ge(S["ev"].h, b["ev"] + 4 * (gidx + 1))
                    dst = h_d[i, 4 * gg:4 * gg + 4].transpose([1, 0, 2])
                    R_hsp.dma(g, dst,
                              bass.AP(gbuf, (gidx % 2) * 4 * TT,
                                      [[GROW, 128], [TT, 4], [1, TT]]))
                    if (first and w2sp < HCH and gidx >= 4
                            and gidx % 2 == 0):
                        a = w2sp
                        g.wait_ge(VCH.h, VC["w2q", a])
                        dst2 = bass.AP(w2q_t, a * 128,
                                       [[HCH * 128, 128],
                                        [128 * HCH * 128, NCH], [1, 128]])
                        src2 = bass.AP(w2qb, (a % 2) * 1024,
                                       [[2048, 128], [128, NCH], [1, 128]])
                        R_w2qsp.dma(g, dst2, src2)
                        w2sp += 1
                while first and w2sp < HCH:
                    a = w2sp
                    g.wait_ge(VCH.h, VC["w2q", a])
                    dst2 = bass.AP(w2q_t, a * 128,
                                   [[HCH * 128, 128],
                                    [128 * HCH * 128, NCH], [1, 128]])
                    src2 = bass.AP(w2qb, (a % 2) * 1024,
                                   [[2048, 128], [128, NCH], [1, 128]])
                    R_w2qsp.dma(g, dst2, src2)
                    w2sp += 1
                g.wait_ge(VCH.h, VC["hm", 8 * NT - 1])
                DMA(g, S["sp"], cc_in[0:1, :].transpose([1, 0]), rmax[:, 0:1])
                g.wait_ge(S["sp"].h, S["sp"].n)
                S["cc"].inc(g.collective_compute(
                    "AllReduce", OP.max, replica_groups=[list(range(8))],
                    ins=[cc_in], outs=[cc_out]))
                if first:
                    awsrc = [winT_r, waT_r, wbT_r, wvT_r, woT_r]

                    def WFL(t_):
                        if t_ >= 2:
                            g.wait_ge(VCH.h, VC["wcv", t_ - 2])
                        w_, hh_ = t_ // 2, t_ % 2
                        src = awsrc[w_][:, 4 * hh_:4 * hh_ + 4, :]
                        R_wfl.dma(g, bass.AP(wfstg, (t_ % 2) * 4096,
                                             [[2 * 4096, 128], [D, 4],
                                              [1, D]]),
                                  src)

                    WFL(0)
                    WFL(1)
                    for t_ in range(10):
                        g.wait_ge(VCH.h, VC["wcv", t_])
                        w_, hh_ = t_ // 2, t_ % 2
                        dst = bass.AP(
                            watt_t, w_ * 128 * 8192 + hh_ * 4096,
                            [[8192, 128], [1, 4096]])
                        R_wfsp.dma(g, dst,
                                   _flat(wfbf, (t_ % 2) * 4096, 4096,
                                         2 * 4096))
                        if t_ + 2 < 10:
                            WFL(t_ + 2)
        es.close()

    def emit_passC(ncc, col, res_src_fn, dst_fn, x0dt, outdt, x0_wait_fn,
                  guard, hsp_base):
        b = snap()
        sfx = "_%d" % ncc
        h_base = R_h.k
        w2b_base = R_w2b.k
        x0_base = R_x0.k
        sp_base = R_x1sp.k
        es = contextlib.ExitStack()
        hbuf = es.enter_context(
            nc.sbuf_tensor("hbuf" + sfx, [128, 3, NCH, TT], F16))
        hqf = es.enter_context(
            nc.sbuf_tensor("hqf" + sfx, [128, NCH, TT], F32))
        hqi = es.enter_context(
            nc.sbuf_tensor("hqi" + sfx, [128, NCH, TT], I32))
        hqbf = es.enter_context(
            nc.sbuf_tensor("hqbf" + sfx, [128, 2, HCH, TT], BF16))
        w2b = es.enter_context(
            nc.sbuf_tensor("w2b" + sfx, [128, 3, HCH, 128], BF16))
        x0b = es.enter_context(
            nc.sbuf_tensor("x0b" + sfx, [128, 2, NCH, TT], x0dt))
        obuf = es.enter_context(
            nc.sbuf_tensor("obuf" + sfx, [128, 2, TT], F32))
        x1b = es.enter_context(
            nc.sbuf_tensor("x1b" + sfx, [128, 2, NCH, TT], outdt))
        with nc.Block("b_ffnC" + str(ncc)) as blk:
            HROW = 3 * NCH * TT
            QROW = 2 * HCH * TT
            WROW = 3 * HCH * 128
            XROW = 2 * NCH * TT
            OROW = 2 * TT
            VC = {}

            @blk.vector
            def _(v):
                n_s = fin_scalar_v(v, b["g"] + 16, row128[:])
                VC["s"] = n_s
                v.wait_ge(S["g"].h, b["g"] + 48)
                C(v.reciprocal(out=scl[:, col + 1:col + 2],
                               in_=scl[:, col:col + 1]))
                C(v.tensor_mul(scl[:, col + 2:col + 3], sw2_b[:, 0:1],
                               scl[:, col:col + 1]))
                C(v.tensor_scalar(scl[:, col + 2:col + 3],
                                  scl[:, col + 2:col + 3], 0.5, None,
                                  OP.mult))
                VC["fin"] = C(v.tensor_scalar(b2s[:], b2sb[:],
                                              scl[:, col + 2:col + 3], None,
                                              OP.mult))

                def QNTH(i, gq):
                    m = 4 * i + gq
                    R_h.wait(v, h_base + m)
                    if gq == 0 and i >= 2:
                        v.wait_ge(S["mm"].h, b["mm"] + NCH * (i - 1))
                    hf = _flat(hbuf, (m % 3) * NCH * TT, NCH * TT, HROW)
                    C(v.tensor_scalar(_flat(hqf, 0, NCH * TT, NCH * TT), hf,
                                      scl[:, col + 1:col + 2], CMAGIC,
                                      OP.mult, OP.add))
                    C(v.tensor_scalar(_flat(hqi, 0, NCH * TT, NCH * TT),
                                      _flat(hqf, 0, NCH * TT, NCH * TT),
                                      CMAGIC, None, OP.subtract))
                    VC["qa", m] = C(v.tensor_copy(
                        out=_flat(hqbf, (i % 2) * HCH * TT + gq * NCH * TT,
                                  NCH * TT, QROW),
                        in_=_flat(hqi, 0, NCH * TT, NCH * TT)))

                def RES(i):
                    for d in range(NCH):
                        m = NCH * i + d
                        v.wait_ge(S["ev"].h, b["ev"] + m + 1)
                        R_x0.wait(v, x0_base + i)
                        if d == 0 and i >= 2:
                            R_x1sp.wait(v, sp_base + i - 2)
                        VC["res", m] = C(v.tensor_add(
                            _flat(x1b, (i % 2) * NCH * TT + d * TT, TT, XROW),
                            _flat(obuf, (d % 2) * TT, TT, OROW),
                            _flat(x0b, (i % 2) * NCH * TT + d * TT, TT,
                                  XROW)))

                for i in range(NT):
                    for gq in range(4):
                        QNTH(i, gq)
                    if i >= 1:
                        RES(i - 1)
                RES(NT - 1)

            @blk.sync
            def _(s):
                barrier(s, guard)
                for i in range(NT):
                    for gq in range(4):
                        m = 4 * i + gq
                        if m >= 3:
                            s.wait_ge(VCH.h, VC["qa", m - 3])
                        R_hsp.wait(s, hsp_base + 8 * i + 2 * gq + 1)
                        src = h_d[i, 8 * gq:8 * gq + 8].transpose([1, 0, 2])
                        R_h.dma(s,
                                bass.AP(hbuf, (m % 3) * NCH * TT,
                                        [[HROW, 128], [TT, NCH], [1, TT]]),
                                src)

            @blk.scalar
            def _(sc):
                def W2S(m):
                    d = m % NCH
                    if m == 0:
                        barrier(sc, guard)
                    if m >= 3:
                        sc.wait_ge(S["mm"].h, b["mm"] + m - 2)
                    R_w2b.dma(sc,
                              bass.AP(w2b, (m % 3) * HCH * 128,
                                      [[WROW, 128], [128, HCH], [1, 128]]),
                              w2q_d[d])

                def OEV(m):
                    d = m % NCH
                    sc.wait_ge(S["mm"].h, b["mm"] + m + 1)
                    if m >= 2:
                        sc.wait_ge(VCH.h, VC["res", m - 2])
                    S["ev"].inc(sc.activation(
                        out=_flat(obuf, (d % 2) * TT, TT, OROW),
                        in_=pb[d % 2][:, 0:TT], func=AF.Identity,
                        scale=scl[:, col + 2:col + 3],
                        bias=b2s[:, d:d + 1]))

                for m in range(3):
                    W2S(m)
                for m in range(NCH * NT):
                    OEV(m)
                    if m + 3 < NCH * NT:
                        W2S(m + 3)

            @blk.tensor
            def _(t):
                t.wait_ge(VCH.h, VC["fin"])
                for i in range(NT):
                    t.wait_ge(VCH.h, VC["qa", 4 * i + 3])
                    for d in range(NCH):
                        m = NCH * i + d
                        R_w2b.wait(t, w2b_base + m)
                        if m >= 2:
                            t.wait_ge(S["ev"].h, b["ev"] + m - 1)
                        pB = pb[d % 2]
                        for j in range(HCH):
                            ins = t.matmul(
                                pB[:, 0:TT],
                                _flat(w2b, (m % 3) * HCH * 128 + j * 128,
                                      128, WROW),
                                _flat(hqbf, (i % 2) * HCH * TT + j * TT,
                                      TT, QROW),
                                start=(j == 0), stop=(j == HCH - 1))
                        S["mm"].inc(ins)

            @blk.gpsimd
            def _(g):
                g.wait_ge(S["cc"].h, ncc)
                DMA(g, S["g"], row128[:], cc_out[0:1, :])
                fin_bcast_g(g, VC["s"], 8 + ncc, scl[:, col:col + 1])
                barrier(g, guard)
                for i in range(NT):
                    x0_wait_fn(g, i)
                    if i >= 2:
                        g.wait_ge(VCH.h, VC["res", NCH * (i - 1) - 1])
                    R_x0.dma(g,
                             bass.AP(x0b, (i % 2) * NCH * TT,
                                     [[XROW, 128], [TT, NCH], [1, TT]]),
                             res_src_fn(i))
                    if i >= 1:
                        j = i - 1
                        g.wait_ge(VCH.h, VC["res", NCH * (j + 1) - 1])
                        R_x1sp.dma(g, dst_fn(j),
                                   bass.AP(x1b, (j % 2) * NCH * TT,
                                           [[XROW, 128], [TT, NCH],
                                            [1, TT]]))
                j = NT - 1
                g.wait_ge(VCH.h, VC["res", NCH * (j + 1) - 1])
                R_x1sp.dma(g, dst_fn(j),
                           bass.AP(x1b, (j % 2) * NCH * TT,
                                   [[XROW, 128], [TT, NCH], [1, TT]]))
                R_x1sp.wait_all(g)
        es.close()

    def emit_attn(guard, x1sp_base):
        with (
            nc.sbuf_tensor("win_sb", [128, NCH, D], BF16) as win_sb,
            nc.sbuf_tensor("wa_sb", [128, NCH, D], BF16) as wa_sb,
            nc.sbuf_tensor("wb_sb", [128, NCH, D], BF16) as wb_sb,
            nc.sbuf_tensor("wv_sb", [128, NCH, D], BF16) as wv_sb,
            nc.sbuf_tensor("wo_sb", [128, NCH, D], BF16) as wo_sb,
        ):
            wdsts = [win_sb, wa_sb, wb_sb, wv_sb, wo_sb]
            wal_base = R_wal.k
            b = snap()
            es = contextlib.ExitStack()

            def lsb(name, shape, dt):
                return es.enter_context(nc.sbuf_tensor(name, shape, dt))

            xa = lsb("xa", [128, 3, NCH, TA], F16)
            xabf = lsb("xabf", [128, 1, NCH, TA], BF16)
            sqbf = lsb("sqbf", [128, 1, NCH, TA], BF16)
            stb = lsb("stb", [128, 1, 4, TA], F32)
            rstd = lsb("rstd", [128, 1, TA], F32)
            lnf = lsb("lnf", [128, NCH, TA], F32)
            ln2bf = lsb("ln2bf", [128, 2, NCH, TA], BF16)
            xpbf = lsb("xpbf", [128, 2, NCH, TA], BF16)
            af32 = lsb("af32", [128, 8, TA], F32)
            bf32 = lsb("bf32", [128, 8, TA], F32)
            vf32 = lsb("vf32", [128, 8, TA], F32)
            bvf = lsb("bvf", [128, 1, TA], F32)
            hb = lsb("hb", [128, 2, NCH, TA], F32)
            sg2 = lsb("sg2", [128, 4, TA], F32)
            cellbf = lsb("cellbf", [128, 2, NCH, TA], BF16)
            x2b = lsb("x2b", [128, 1, NCH, TA], F16)
            xbf2 = lsb("xbf2", [128, 1, NCH, TA], BF16)
            sqbf2 = lsb("sqbf2", [128, 1, NCH, TA], BF16)
            stb3 = lsb("stb3", [128, 1, 4, TA], F32)
            rstd3 = lsb("rstd3", [128, 1, TA], F32)
            ln3b = lsb("ln3b", [128, 1, NCH, TA], F16)
            with nc.Block("b6b") as blk:
                XROW = 2 * NCH * TA
                X1ROW = NCH * TA
                SROW = 4 * TA
                RROW = 2 * TA
                FROW = NCH * TA
                VC = {}

                rowlens = {id(xa): 3 * NCH * TA, id(xabf): X1ROW, id(sqbf): X1ROW,
                           id(ln2bf): XROW, id(xpbf): XROW, id(hb): XROW,
                           id(cellbf): XROW, id(x2b): X1ROW, id(xbf2): X1ROW,
                           id(sqbf2): X1ROW, id(ln3b): X1ROW, id(lnf): FROW}

                def f3(th, k):
                    return bass.AP(th, k * NCH * TA,
                                   [[rowlens[id(th)], 128], [TA, NCH],
                                    [1, TA]])

                def ff(th, k, n=NCH * TA, extra=0):
                    return _flat(th, k * NCH * TA + extra, n,
                                 rowlens[id(th)])

                def stslot(th, slot):
                    return _flat(th, slot * TA, TA, SROW)

                def stslot_b(th, slot):
                    return bass.AP(th, slot * TA,
                                   [[SROW, 128], [0, NCH], [1, TA]])

                def rstd_bc(th):
                    return bass.AP(th, 0, [[TA, 128], [0, NCH], [1, TA]])

                def half(th, k):
                    return _flat(th, k * TA, TA, RROW)

                def half4(th, k):
                    return _flat(th, k * TA, TA, 4 * TA)

                def half8(th, k):
                    return _flat(th, k * TA, TA, 8 * TA)

                def psl(bank, j):
                    return pb[bank][:, j * TA:(j + 1) * TA]

                lnf3 = bass.AP(lnf, 0, [[FROW, 128], [TA, NCH], [1, TA]])

                @blk.vector
                def _(v):
                    C(v.memset(rmax[:], 0.0))

                    def PRE(k):
                        R_xa.wait(v, k)
                        if k >= 1:
                            v.wait_ge(S["mm2"].h, b["mm2"] + k)
                        VC["pre", k] = C(v.tensor_copy(out=ff(xabf, 0),
                                                       in_=ff(xa, k % 3)))

                    def TAILS(k):
                        v.wait_ge(S["mm2"].h, b["mm2"] + k + 1)
                        C(v.tensor_scalar(stslot(stb, 0), psl(6, 0),
                                          1.0 / D, None, OP.mult))
                        C(v.tensor_scalar(stslot(stb, 1), psl(6, 1),
                                          1.0 / D, None, OP.mult))
                        C(v.tensor_mul(stslot(stb, 3), stslot(stb, 0),
                                       stslot(stb, 0)))
                        VC["var", k] = C(v.tensor_sub(stslot(stb, 2),
                                                      stslot(stb, 1),
                                                      stslot(stb, 3)))
                        v.wait_ge(S["sd"].h, b["sd"] + k + 1)
                        C(v.reciprocal(out=_flat(rstd, 0, TA, TA),
                                       in_=stslot(stb, 3)))
                        if k >= 2:
                            v.wait_ge(S["mi"].h, b["mi"] + NCH * (k - 1))
                        C(v.tensor_tensor(out=lnf3, in0=f3(xa, k % 3),
                                          in1=stslot_b(stb, 0),
                                          op=OP.subtract))
                        VC["qa", k] = C(v.tensor_tensor(
                            out=f3(ln2bf, k % 2), in0=lnf3,
                            in1=rstd_bc(rstd), op=OP.mult))

                    PRE(0)
                    TAILS(0)
                    if NA > 1:
                        PRE(1)
                    for k in range(NA):
                        if k + 1 < NA:
                            TAILS(k + 1)
                        if k + 2 < NA:
                            PRE(k + 2)
                        for f in range(NCH):
                            mf = NCH * k + f
                            v.wait_ge(S["bee"].h, b["bee"] + mf + 1)
                            v.wait_ge(S["vee"].h, b["vee"] + mf + 1)
                            VC["bv", mf] = C(v.tensor_mul(
                                _flat(bvf, 0, TA, TA), half8(bf32, f),
                                half8(vf32, f)))
                            v.wait_ge(S["ale"].h, b["ale"] + mf + 1)
                            init = (0.0 if k == 0 else
                                    bass.AP(hb, (1 - k % 2) * NCH * TA
                                            + f * TA + TA - 1,
                                            [[XROW, 128], [1, 1]]))
                            VC["scan", mf] = C(v.tensor_tensor_scan(
                                ff(hb, k % 2, TA, f * TA),
                                half8(af32, f),
                                _flat(bvf, 0, TA, TA), init,
                                OP.mult, OP.add))
                        for f in range(NCH):
                            mf = NCH * k + f
                            v.wait_ge(S["sg"].h, b["sg"] + mf + 1)
                            if f == 0 and k >= 2:
                                v.wait_ge(S["mo"].h, b["mo"] + NCH * (k - 1))
                            VC["cell", mf] = C(v.tensor_mul(
                                ff(cellbf, k % 2, TA, f * TA),
                                ff(hb, k % 2, TA, f * TA),
                                half4(sg2, f % 4)))
                        for d in range(NCH):
                            md = NCH * k + d
                            v.wait_ge(S["mo"].h, b["mo"] + md + 1)
                            if d == 0 and k >= 1:
                                R_x2sp.wait(v, k - 1)
                            VC["x2", md] = C(v.tensor_add(
                                ff(x2b, 0, TA, d * TA), pb[1][:, 0:TA],
                                ff(xa, k % 3, TA, d * TA)))
                        if k >= 1:
                            v.wait_ge(S["mm3"].h, b["mm3"] + k)
                        VC["a3", k] = C(v.tensor_copy(out=ff(xbf2, 0),
                                                      in_=ff(x2b, 0)))
                        VC["sq2v", k] = C(v.tensor_mul(
                            ff(sqbf2, 0), ff(x2b, 0), ff(x2b, 0)))
                        v.wait_ge(S["mm3"].h, b["mm3"] + k + 1)
                        C(v.tensor_scalar(stslot(stb3, 0), psl(7, 0),
                                          1.0 / D, None, OP.mult))
                        C(v.tensor_scalar(stslot(stb3, 1), psl(7, 1),
                                          1.0 / D, None, OP.mult))
                        C(v.tensor_mul(stslot(stb3, 3), stslot(stb3, 0),
                                       stslot(stb3, 0)))
                        VC["var3", k] = C(v.tensor_sub(stslot(stb3, 2),
                                                       stslot(stb3, 1),
                                                       stslot(stb3, 3)))
                        v.wait_ge(S["sd3"].h, b["sd3"] + k + 1)
                        C(v.reciprocal(out=_flat(rstd3, 0, TA, TA),
                                       in_=stslot(stb3, 3)))
                        if k >= 1:
                            R_ln3sp.wait(v, k - 1)
                        C(v.tensor_tensor(out=lnf3, in0=f3(x2b, 0),
                                          in1=stslot_b(stb3, 0),
                                          op=OP.subtract))
                        C(v.tensor_tensor(
                            out=f3(ln3b, 0), in0=lnf3,
                            in1=rstd_bc(rstd3), op=OP.mult))
                        C(v.reduce_max(out=tmax[:], in_=ff(ln3b, 0),
                                       axis=AX.X,
                                       apply_absolute_value=True))
                        VC["ln3", k] = C(v.tensor_tensor(
                            out=rmax[:], in0=rmax[:], in1=tmax[:],
                            op=OP.max))

                @blk.sync
                def _(s):
                    barrier(s, guard)
                    for w in range(5):
                        R_wfsp.wait(s, 2 * w + 1)
                        R_wal.dma(s, _flat(wdsts[w], 0, NCH * D, NCH * D),
                                  watt_d[w])
                    for i in range(NA):
                        if i >= 3:
                            s.wait_ge(VCH.h, VC["x2", NCH * (i - 2) - 1])
                            s.wait_ge(S["sq"].h, b["sq"] + i - 2)
                        R_x1sp.wait(s, x1sp_base + i // 2)
                        src = x1_d[:, :, i * TA:(i + 1) * TA].transpose(
                            [1, 0, 2])
                        R_xa.dma(s, f3(xa, i % 3), src)

                @blk.scalar
                def _(sc):
                    def SQA(k):
                        R_xa.wait(sc, k)
                        if k >= 1:
                            sc.wait_ge(S["mm2"].h, b["mm2"] + k)
                        S["sq"].inc(sc.activation(out=ff(sqbf, 0),
                                                  in_=ff(xa, k % 3),
                                                  func=AF.Square))

                    def SDA(k):
                        sc.wait_ge(VCH.h, VC["var", k])
                        S["sd"].inc(sc.activation(out=stslot(stb, 3),
                                                  in_=stslot(stb, 2),
                                                  func=AF.Sqrt, bias=EPS_A))

                    def SQ2(k):
                        sc.wait_ge(VCH.h, VC["var3", k])
                        S["sd3"].inc(sc.activation(out=stslot(stb3, 3),
                                                   in_=stslot(stb3, 2),
                                                   func=AF.Sqrt, bias=EPS_A))

                    SQA(0)
                    SDA(0)
                    for k in range(NA):
                        for e in range(NCH):
                            me = NCH * k + e
                            sc.wait_ge(S["mi"].h, b["mi"] + me + 1)
                            if k >= 2 and e == 0:
                                sc.wait_ge(S["pv"].h, b["pv"] + NCH * (k - 1))
                            S["xps"].inc(sc.activation(
                                out=ff(xpbf, k % 2, TA, e * TA),
                                in_=pb[2 + e % 2][:, 0:TA], func=AF.Silu))
                        if k + 1 < NA:
                            SQA(k + 1)
                        if k >= 1:
                            SQ2(k - 1)
                        if k + 1 < NA:
                            SDA(k + 1)
                        for g in range(24):
                            proj, f = g // 8, g % 8
                            mf = NCH * k + f
                            bank = (4, 5, 0)[g % 3]
                            sc.wait_ge(S["pv"].h, b["pv"] + 24 * k + g + 1)
                            if proj == 0:
                                if k >= 1:
                                    sc.wait_ge(VCH.h, VC["scan", mf - 8])
                                S["ale"].inc(sc.activation(
                                    out=half8(af32, f),
                                    in_=pb[bank][:, 0:TA],
                                    func=AF.Sigmoid, bias=bap_s[:, f:f + 1]))
                            elif proj == 1:
                                if k >= 1:
                                    sc.wait_ge(VCH.h, VC["bv", mf - 8])
                                S["bee"].inc(sc.activation(
                                    out=half8(bf32, f),
                                    in_=pb[bank][:, 0:TA],
                                    func=AF.Sigmoid, bias=bbp_s[:, f:f + 1]))
                            else:
                                if k >= 1:
                                    sc.wait_ge(VCH.h, VC["bv", mf - 8])
                                S["vee"].inc(sc.activation(
                                    out=half8(vf32, f),
                                    in_=pb[bank][:, 0:TA],
                                    func=AF.Tanh, bias=bvp_s[:, f:f + 1]))
                        for f in range(NCH):
                            mf = NCH * k + f
                            sc.wait_ge(VCH.h, VC["scan", mf])
                            if mf >= 4:
                                sc.wait_ge(VCH.h, VC["cell", mf - 4])
                            S["sg"].inc(sc.activation(
                                out=half4(sg2, f % 4),
                                in_=ff(hb, k % 2, TA, f * TA),
                                func=AF.Silu))
                    SQ2(NA - 1)

                @blk.tensor
                def _(t):
                    def STATS(k):
                        t.wait_ge(VCH.h, VC["pre", k])
                        t.wait_ge(S["sq"].h, b["sq"] + k + 1)
                        if k >= 1:
                            t.wait_ge(VCH.h, VC["var", k - 1])
                        for c in range(NCH):
                            t.matmul(psl(6, 0), ones_bf[:],
                                     ff(xabf, 0, TA, c * TA),
                                     start=(c == 0), stop=(c == NCH - 1))
                        for c in range(NCH):
                            ins = t.matmul(psl(6, 1), ones_bf[:],
                                           ff(sqbf, 0, TA, c * TA),
                                           start=(c == 0),
                                           stop=(c == NCH - 1))
                        S["mm2"].inc(ins)

                    def MI(k):
                        t.wait_ge(VCH.h, VC["qa", k])
                        for e in range(NCH):
                            me = NCH * k + e
                            if e >= 2:
                                t.wait_ge(S["xps"].h, b["xps"] + me - 1)
                            for c in range(NCH):
                                ins = t.matmul(
                                    pb[2 + e % 2][:, 0:TA],
                                    bass.AP(win_sb, c * D + e * 128,
                                            [[NCH * D, 128], [1, 128]]),
                                    ff(ln2bf, k % 2, TA, c * TA),
                                    start=(c == 0), stop=(c == NCH - 1))
                            S["mi"].inc(ins)

                    PBK = (4, 5, 0)
                    PSEM = ("ale", "bee", "vee")
                    PWSB = None

                    def PV(k):
                        t.wait_ge(S["xps"].h, b["xps"] + NCH * (k + 1))
                        wsbs = (wa_sb, wb_sb, wv_sb)
                        for g in range(24):
                            proj, f = g // 8, g % 8
                            bank = PBK[g % 3]
                            pg = g - 3
                            if pg >= 0:
                                pproj, pf = pg // 8, pg % 8
                                t.wait_ge(S[PSEM[pproj]].h,
                                          b[PSEM[pproj]] + NCH * k + pf + 1)
                            elif k >= 1:
                                pg2 = pg + 24
                                pproj, pf = pg2 // 8, pg2 % 8
                                t.wait_ge(S[PSEM[pproj]].h,
                                          b[PSEM[pproj]]
                                          + NCH * (k - 1) + pf + 1)
                            for c in range(NCH):
                                ins = t.matmul(
                                    pb[bank][:, 0:TA],
                                    bass.AP(wsbs[proj], c * D + f * 128,
                                            [[NCH * D, 128], [1, 128]]),
                                    ff(xpbf, k % 2, TA, c * TA),
                                    start=(c == 0), stop=(c == NCH - 1))
                            S["pv"].inc(ins)

                    def MO(k):
                        t.wait_ge(VCH.h, VC["cell", NCH * (k + 1) - 1])
                        for d in range(NCH):
                            md = NCH * k + d
                            if md >= 1:
                                t.wait_ge(VCH.h, VC["x2", md - 1])
                            for c in range(NCH):
                                ins = t.matmul(
                                    pb[1][:, 0:TA],
                                    bass.AP(wo_sb, c * D + d * 128,
                                            [[NCH * D, 128], [1, 128]]),
                                    ff(cellbf, k % 2, TA, c * TA),
                                    start=(c == 0), stop=(c == NCH - 1))
                            S["mo"].inc(ins)

                    def STATS3(k):
                        t.wait_ge(VCH.h, VC["sq2v", k])
                        if k >= 1:
                            t.wait_ge(VCH.h, VC["var3", k - 1])
                        for c in range(NCH):
                            t.matmul(psl(7, 0), ones_bf[:],
                                     ff(xbf2, 0, TA, c * TA),
                                     start=(c == 0), stop=(c == NCH - 1))
                        for c in range(NCH):
                            ins = t.matmul(psl(7, 1), ones_bf[:],
                                           ff(sqbf2, 0, TA, c * TA),
                                           start=(c == 0),
                                           stop=(c == NCH - 1))
                        S["mm3"].inc(ins)

                    STATS(0)
                    R_wal.wait(t, wal_base + 0)
                    MI(0)
                    for k in range(NA):
                        if k + 1 < NA:
                            STATS(k + 1)
                        if k == 0:
                            for w in range(1, 4):
                                R_wal.wait(t, wal_base + w)
                        PV(k)
                        if k + 1 < NA:
                            MI(k + 1)
                        if k == 0:
                            R_wal.wait(t, wal_base + 4)
                        MO(k)
                        STATS3(k)

                @blk.gpsimd
                def _(g):
                    barrier(g, guard)
                    for i in range(NA):
                        g.wait_ge(VCH.h, VC["x2", NCH * (i + 1) - 1])
                        dst = x2_d[:, :, i * TA:(i + 1) * TA].transpose(
                            [1, 0, 2])
                        R_x2sp.dma(g, dst, f3(x2b, 0))
                        g.wait_ge(VCH.h, VC["ln3", i])
                        dst2 = ln_d[:, :, i * TA:(i + 1) * TA].transpose(
                            [1, 0, 2])
                        R_ln3sp.dma(g, dst2, f3(ln3b, 0))
                    DMA(g, S["sp"], cc_in[0:1, :].transpose([1, 0]),
                        rmax[:, 0:1])
                    g.wait_ge(S["sp"].h, S["sp"].n)
                    S["cc"].inc(g.collective_compute(
                        "AllReduce", OP.max, replica_groups=[list(range(8))],
                        ins=[cc_in], outs=[cc_out]))
            es.close()

    with nc.sbuf_tensor("w1q_sb", [128, NCH, H], BF16) as w1q_sb:
        w1q_vc = emit_b12(w1q_sb)
        g12 = take_guard()
        hsp1_base = R_hsp.k
        emit_passB(ncc=2, col=0,
                   w1q_ap=lambda c, j: bass.AP(
                       w1q_sb, c * H + j * 128, [[NCH * H, 128], [1, 128]]),
                   first=True, guard=g12,
                   ln_wait_fn=lambda s, i: R_lnsp.wait(s, i),
                   w1q_vc=w1q_vc, w1q_sb=w1q_sb)
    gB1 = take_guard()
    x1sp_base = R_x1sp.k

    def x0w_c2(g, i):
        pass

    emit_passC(ncc=3, col=3,
               res_src_fn=lambda i: xT_r[:, :, i * TT:(i + 1) * TT],
               dst_fn=lambda i: x1_d[:, :, i * TT:(i + 1) * TT]
               .transpose([1, 0, 2]),
               x0dt=F32, outdt=F16, x0_wait_fn=x0w_c2,
               guard=gB1, hsp_base=hsp1_base)
    gC2 = take_guard()
    emit_attn(guard=gC2, x1sp_base=x1sp_base)
    gAT = take_guard()

    with nc.sbuf_tensor("w1q_sb3", [128, NCH, H], BF16) as w1q_sb3:
        with nc.Block("b_w1ql") as blk:
            @blk.sync
            def _(s):
                barrier(s, gAT)
                for j in range(HCH):
                    R_w1qsp.wait(s, j)
                    dst = _flat(w1q_sb3, j * NCH * 128, NCH * 128,
                                HCH * NCH * 128)
                    src = bass.AP(w1q_t, j * 128 * NCH * 128,
                                  [[NCH * 128, 128], [1, NCH * 128]])
                    R_w1ql.dma(s, dst, src)
        hsp3_base = R_hsp.k
        emit_passB(ncc=4, col=6,
                   w1q_ap=lambda c, j: bass.AP(
                       w1q_sb3, j * NCH * 128 + c * 128,
                       [[HCH * NCH * 128, 128], [1, 128]]),
                   first=False, guard=gAT,
                   ln_wait_fn=lambda s, i: R_ln3sp.wait(s, 2 * i + 1),
                   w1q_vc=0)
    gB3 = take_guard()

    def x0w_c4(g, i):
        R_x2sp.wait(g, 2 * i + 1)

    emit_passC(ncc=5, col=9,
               res_src_fn=lambda i: x2_d[:, :, i * TT:(i + 1) * TT]
               .transpose([1, 0, 2]),
               dst_fn=lambda i: outT_r[:, :, i * TT:(i + 1) * TT],
               x0dt=F16, outdt=F32, x0_wait_fn=x0w_c4,
               guard=gB3, hsp_base=hsp3_base)

    nc.finalize()
    st.close()
    return nc


def kernel(L0_attn_x, L0_attn_W_in, L0_attn_W_alpha, L0_attn_b_alpha,
           L0_attn_W_beta, L0_attn_b_beta, L0_attn_W_v, L0_attn_b_v,
           L0_attn_W_out, L0_ffn_w1, L0_ffn_b1, L0_ffn_w2, L0_ffn_b2):
    f32 = np.float32
    x = np.asarray(L0_attn_x, dtype=f32)
    w1T = np.ascontiguousarray(np.asarray(L0_ffn_w1, f32).T)
    w2T = np.ascontiguousarray(np.asarray(L0_ffn_w2, f32).T)
    winT = np.ascontiguousarray(np.asarray(L0_attn_W_in, f32).T)
    waT = np.ascontiguousarray(np.asarray(L0_attn_W_alpha, f32).T)
    wbT = np.ascontiguousarray(np.asarray(L0_attn_W_beta, f32).T)
    wvT = np.ascontiguousarray(np.asarray(L0_attn_W_v, f32).T)
    woT = np.ascontiguousarray(np.asarray(L0_attn_W_out, f32).T)
    b1pm = np.ascontiguousarray(np.asarray(L0_ffn_b1, f32).reshape(HCH, 128).T)
    b2pm = np.ascontiguousarray(np.asarray(L0_ffn_b2, f32).reshape(NCH, 128).T)
    bapm = np.ascontiguousarray(
        np.asarray(L0_attn_b_alpha, f32).reshape(NCH, 128).T)
    bbpm = np.ascontiguousarray(
        np.asarray(L0_attn_b_beta, f32).reshape(NCH, 128).T)
    bvpm = np.ascontiguousarray(
        np.asarray(L0_attn_b_v, f32).reshape(NCH, 128).T)
    w1flat = w1T.reshape(-1)
    w2flat = w2T.reshape(-1)
    shl = w1flat.size // 8

    nc = build_program()
    in_maps = []
    for bb in range(8):
        wshard = np.ascontiguousarray(np.concatenate(
            [w1flat[bb * shl:(bb + 1) * shl].reshape(128, 4096),
             w2flat[bb * shl:(bb + 1) * shl].reshape(128, 4096)], axis=1))
        in_maps.append({
            "xT": np.ascontiguousarray(x[bb].T),
            "w1T": w1T, "w2T": w2T, "wsh": wshard,
            "winT": winT, "waT": waT, "wbT": wbT, "wvT": wvT, "woT": woT,
            "b1p": b1pm, "b2p": b2pm,
            "bap": bapm, "bbp": bbpm, "bvp": bvpm,
        })
    res = run_bass_kernel_spmd(nc, in_maps, core_ids=list(range(8)))
    out = np.stack([np.asarray(res.results[bb]["outT"]).T for bb in range(8)])
    return out.astype(np.float32)


# version 12
# speedup vs baseline: 1.0200x; 1.0200x over previous
import contextlib
import numpy as np
import concourse.bass as bass
import concourse.mybir as mybir
from concourse.bass_utils import run_bass_kernel_spmd

F32 = mybir.dt.float32
F16 = mybir.dt.float16
BF16 = mybir.dt.bfloat16
I32 = mybir.dt.int32
AF = mybir.ActivationFunctionType
OP = mybir.AluOpType
AX = mybir.AxisListType

D = 1024
T = 4096
H = 4096
NCH = 8
HCH = 32
TT = 512
NT = T // TT
TA = 256
NA = T // TA
CMAGIC = 12582912.0
EPS = 1e-5
QP = 127.0
MINR = 1e-6


class Sem:
    def __init__(self, stack, nc, name):
        self.h = stack.enter_context(nc.semaphore(name))
        self.n = 0

    def inc(self, inst, k=1):
        inst.then_inc(self.h, k)
        self.n += k
        return self.n


class Ring:

    def __init__(self, stack, nc, name, depth):
        self.sems = [Sem(stack, nc, f"{name}{k}") for k in range(depth)]
        self.depth = depth
        self.k = 0

    def dma(self, eng, out, in_):
        d, k = self.depth, self.k
        s = self.sems[k % d]
        if k >= d:
            eng.wait_ge(s.h, 16 * (k // d))
        s.inc(eng.dma_start(out=out, in_=in_), 16)
        self.k += 1
        return k

    def wv(self, k):
        d = self.depth
        return self.sems[k % d].h, 16 * (k // d + 1)

    def wait(self, eng, k):
        h, v = self.wv(k)
        eng.wait_ge(h, v)

    def wait_all(self, eng):
        for s in self.sems:
            if s.n:
                eng.wait_ge(s.h, s.n)


def _flat(th, off, n, rowlen):
    return bass.AP(th, off, [[rowlen, 128], [1, n]])


def build_program():
    nc = bass.Bass("TRN2", target_bir_lowering=False, debug=False)

    xT = nc.dram_tensor("xT", [D, T], F32, kind="ExternalInput").ap()
    w1T = nc.dram_tensor("w1T", [D, H], F32, kind="ExternalInput").ap()
    w2T = nc.dram_tensor("w2T", [H, D], F32, kind="ExternalInput").ap()
    wsh = nc.dram_tensor("wsh", [128, 8192], F32, kind="ExternalInput").ap()
    winT = nc.dram_tensor("winT", [D, D], F32, kind="ExternalInput").ap()
    waT = nc.dram_tensor("waT", [D, D], F32, kind="ExternalInput").ap()
    wbT = nc.dram_tensor("wbT", [D, D], F32, kind="ExternalInput").ap()
    wvT = nc.dram_tensor("wvT", [D, D], F32, kind="ExternalInput").ap()
    woT = nc.dram_tensor("woT", [D, D], F32, kind="ExternalInput").ap()
    b1p = nc.dram_tensor("b1p", [128, HCH], F32, kind="ExternalInput").ap()
    b2p = nc.dram_tensor("b2p", [128, NCH], F32, kind="ExternalInput").ap()
    bap = nc.dram_tensor("bap", [128, NCH], F32, kind="ExternalInput").ap()
    bbp = nc.dram_tensor("bbp", [128, NCH], F32, kind="ExternalInput").ap()
    bvp = nc.dram_tensor("bvp", [128, NCH], F32, kind="ExternalInput").ap()
    outT = nc.dram_tensor("outT", [D, T], F32, kind="ExternalOutput").ap()

    ln_d = nc.dram_tensor("ln_d", [NCH, 128, T], F16, kind="Internal").ap()
    x1_d = nc.dram_tensor("x1_d", [NCH, 128, T], F16, kind="Internal").ap()
    x2_d = nc.dram_tensor("x2_d", [NCH, 128, T], F16, kind="Internal").ap()
    h_d = nc.dram_tensor("h_d", [NT, HCH, 128, TT], F16, kind="Internal").ap()
    w1q_t = nc.dram_tensor("w1q_d", [HCH, 128, NCH * 128], BF16,
                           kind="Internal")
    w1q_d = w1q_t.ap()
    w2q_t = nc.dram_tensor("w2q_d", [NCH, 128, HCH, 128], BF16,
                           kind="Internal")
    w2q_d = w2q_t.ap()

    def w1qd_ap(j):
        return bass.AP(w1q_t, j * 128 * NCH * 128,
                       [[NCH * 128, 128], [128, NCH], [1, 128]])
    sbounce_t = nc.dram_tensor("sbounce", [1, 16], F32, kind="Internal")
    sbounce = sbounce_t.ap()
    watt_t = nc.dram_tensor("watt_d", [5, 128, 8192], BF16,
                            kind="Internal")
    watt_d = watt_t.ap()
    cc_in = nc.dram_tensor("cc_in", [1, 128], F32, kind="Internal").ap()
    cc_out = nc.dram_tensor("cc_out", [1, 128], F32, kind="Internal").ap()
    ccw_in = nc.dram_tensor("ccw_in", [1, 256], F32, kind="Internal").ap()
    ccw_out = nc.dram_tensor("ccw_out", [1, 256], F32, kind="Internal").ap()

    xT_r = xT.rearrange("(c p) t -> p c t", p=128)
    w1T_r = w1T.rearrange("(c p) j -> p c j", p=128)
    w2T_r = w2T.rearrange("(a p) d -> p a d", p=128)
    outT_r = outT.rearrange("(c p) t -> p c t", p=128)
    winT_r = winT.rearrange("(c p) e -> p c e", p=128)
    waT_r = waT.rearrange("(c p) e -> p c e", p=128)
    wbT_r = wbT.rearrange("(c p) e -> p c e", p=128)
    wvT_r = wvT.rearrange("(c p) e -> p c e", p=128)
    woT_r = woT.rearrange("(c p) e -> p c e", p=128)

    st = contextlib.ExitStack()

    def sb(name, shape, dt):
        return st.enter_context(nc.sbuf_tensor(name, shape, dt))

    ones_bf = sb("ones_bf", [128, 128], BF16)
    cst = sb("cst", [128, 4], F32)
    EPS_A = cst[:, 0:1]
    CP_A = cst[:, 1:2]
    CN_A = cst[:, 2:3]
    b1sb = sb("b1sb", [128, HCH], F32)
    b2sb = sb("b2sb", [128, NCH], F32)
    b1s = sb("b1s", [128, HCH], F32)
    b2s = sb("b2s", [128, NCH], F32)
    bap_s = sb("bap_s", [128, NCH], F32)
    bbp_s = sb("bbp_s", [128, NCH], F32)
    bvp_s = sb("bvp_s", [128, NCH], F32)
    sw1_b = sb("sw1_b", [128, 1], F32)
    sw2_b = sb("sw2_b", [128, 1], F32)
    invw1 = sb("invw1", [128, 1], F32)
    invw2 = sb("invw2", [128, 1], F32)
    scl = sb("scl", [128, 12], F32)
    s11 = sb("s11", [1, 2], F32)
    row128 = sb("row128", [1, 128], F32)
    row256 = sb("row256", [1, 256], F32)
    rmax = sb("rmax", [128, 1], F32)
    tmax = sb("tmax", [128, 1], F32)

    pb = [st.enter_context(nc.psum_tensor(f"pb{i}", [128, 512], F32))
          for i in range(8)]

    S = {}
    for name in ["ld", "ld2", "sp", "g", "cc", "vch", "wact", "w2act",
                 "sq", "mm2", "sd", "mm", "ev",
                 "sq2", "mm3", "sd3",
                 "mi", "xps", "ale", "bee", "vee", "pv", "sg", "mo"]:
        S[name] = Sem(st, nc, "s_" + name)

    VCH = S["vch"]

    def C(ins):
        if VCH.n > 0:
            ins.wait_op(VCH.h, VCH.n, "sem-ge")
        VCH.inc(ins)
        return VCH.n

    def snap():
        return {k: s.n for k, s in S.items()}

    def DMA(eng, sem, out, in_):
        if sem.n > 0:
            eng.wait_ge(sem.h, sem.n)
        return sem.inc(eng.dma_start(out=out, in_=in_), 16)

    R_x = Ring(st, nc, "r_x", 4)
    R_w1f = Ring(st, nc, "r_w1f", 2)
    R_lnsp = Ring(st, nc, "r_lnsp", 2)
    R_lb = Ring(st, nc, "r_lb", 4)
    R_w2f = Ring(st, nc, "r_w2f", 2)
    R_hsp = Ring(st, nc, "r_hsp", 4)
    R_w1qsp = Ring(st, nc, "r_w1qsp", 2)
    R_w2qsp = Ring(st, nc, "r_w2qsp", 4)
    R_h = Ring(st, nc, "r_h", 4)
    R_w2b = Ring(st, nc, "r_w2b", 2)
    R_x0 = Ring(st, nc, "r_x0", 2)
    R_x1sp = Ring(st, nc, "r_x1sp", 2)
    R_xa = Ring(st, nc, "r_xa", 4)
    R_x2sp = Ring(st, nc, "r_x2sp", 2)
    R_ln3sp = Ring(st, nc, "r_ln3sp", 2)
    R_w1ql = Ring(st, nc, "r_w1ql", 4)
    R_wfl = Ring(st, nc, "r_wfl", 2)
    R_wfsp = Ring(st, nc, "r_wfsp", 2)
    R_wal = Ring(st, nc, "r_wal", 2)
    RINGS = [R_x, R_w1f, R_lnsp, R_lb, R_w2f, R_hsp, R_w1qsp, R_w2qsp,
             R_h, R_w2b, R_x0, R_x1sp, R_xa, R_x2sp, R_ln3sp, R_w1ql,
             R_wfl, R_wfsp, R_wal]

    def take_guard():
        g = []
        for s in S.values():
            if s.n > 0:
                g.append((s.h, s.n))
        for r in RINGS:
            for s in r.sems:
                if s.n > 0:
                    g.append((s.h, s.n))
        return g

    def barrier(eng, g):
        for h, v in g:
            eng.wait_ge(h, v)

    inv_n = {}

    def fin_scalar_v(v, gwait, src):
        v.wait_ge(S["g"].h, gwait)
        C(v.reduce_max(out=s11[0:1, 0:1], in_=src, axis=AX.X))
        C(v.tensor_scalar(s11[0:1, 0:1], s11[0:1, 0:1], MINR, None, OP.max))
        return C(v.tensor_scalar(s11[0:1, 0:1], s11[0:1, 0:1], 1.0 / QP,
                                 None, OP.mult))

    def fin_bcast_g(g, n_s, scol, dst):
        g.wait_ge(VCH.h, n_s)
        DMA(g, S["g"], sbounce[0:1, scol:scol + 1], s11[0:1, 0:1])
        DMA(g, S["g"], dst, bass.AP(sbounce_t, scol, [[0, 128], [1, 1]]))

    with nc.Block("b0") as blk:
        @blk.vector
        def _(v):
            C(v.memset(ones_bf[:], 1.0))
            C(v.memset(cst[:, 0:1], EPS))
            C(v.memset(cst[:, 1:2], CMAGIC))
            C(v.memset(cst[:, 2:3], -CMAGIC))
            C(v.memset(rmax[:], 0.0))

        @blk.sync
        def _(s):
            DMA(s, S["ld"], b1sb[:], b1p)
            DMA(s, S["ld"], b2sb[:], b2p)
            DMA(s, S["ld"], bap_s[:], bap)
            DMA(s, S["ld"], bbp_s[:], bbp)
            DMA(s, S["ld"], bvp_s[:], bvp)
            s.wait_ge(S["ld"].h, S["ld"].n)

    def emit_b12(w1q_sb):
        b = snap()
        with (
            nc.sbuf_tensor("a_x", [128, 3, NCH, TT], F32) as a_x,
            nc.sbuf_tensor("a_xbf", [128, 1, NCH, TT], BF16) as a_xbf,
            nc.sbuf_tensor("a_sqbf", [128, 2, NCH, TT], BF16) as a_sqbf,
            nc.sbuf_tensor("a_lnh", [128, 2, NCH, TT], F16) as a_lnh,
            nc.sbuf_tensor("a_st", [128, 1, 4, TT], F32) as a_st,
            nc.sbuf_tensor("a_rstd", [128, 1, TT], F32) as a_rstd,
            nc.sbuf_tensor("wshb", [128, 2, 2048], F32) as wshb,
            nc.sbuf_tensor("wstage", [128, 2, 1024], F32) as wstage,
            nc.sbuf_tensor("wqi1", [128, 4, 1024], I32) as wqi1,
            nc.sbuf_tensor("rmw1", [128, 1], F32) as rmw1,
            nc.sbuf_tensor("rmw2", [128, 1], F32) as rmw2,
            nc.Block("b12") as blk,
        ):
            NROW = 3 * NCH * TT
            XBROW = NCH * TT
            SROW = 4 * TT
            RROW = TT
            WSROW = 2 * 2048
            WROW = 2 * 1024
            QROW = 4 * 1024
            VC = {}

            def xf(i):
                return _flat(a_x, (i % 3) * NCH * TT, NCH * TT, NROW)

            def x3(i):
                return bass.AP(a_x, (i % 3) * NCH * TT,
                               [[NROW, 128], [TT, NCH], [1, TT]])

            def stt(k, slot):
                return _flat(a_st, slot * TT, TT, SROW)

            def stt_b(k, slot):
                return bass.AP(a_st, slot * TT,
                               [[SROW, 128], [0, NCH], [1, TT]])

            def rstd_b(k):
                return bass.AP(a_rstd, 0, [[RROW, 128], [0, NCH],
                                           [1, TT]])

            def wshl(item):
                return _flat(wshb, (item % 2) * 2048, 2048, WSROW)

            def wsl(c):
                return _flat(wstage, (c % 2) * 1024, 1024, WROW)

            def wqs(c):
                return _flat(wqi1, (c % 4) * 1024, 1024, QROW)

            @blk.vector
            def _(v):
                R_w1f.wait(v, 0)
                VC["wm", 0] = C(v.reduce_max(out=rmw1[:], in_=wshl(0),
                                             axis=AX.X,
                                             apply_absolute_value=True))
                R_w1f.wait(v, 1)
                C(v.reduce_max(out=tmax[:], in_=wshl(1), axis=AX.X,
                               apply_absolute_value=True))
                VC["wm", 1] = C(v.tensor_tensor(out=rmw1[:], in0=rmw1[:],
                                                in1=tmax[:], op=OP.max))
                R_w1f.wait(v, 2)
                VC["wm", 2] = C(v.reduce_max(out=rmw2[:], in_=wshl(2),
                                             axis=AX.X,
                                             apply_absolute_value=True))
                R_w1f.wait(v, 3)
                C(v.reduce_max(out=tmax[:], in_=wshl(3), axis=AX.X,
                               apply_absolute_value=True))
                VC["wm", 3] = C(v.tensor_tensor(out=rmw2[:], in0=rmw2[:],
                                                in1=tmax[:], op=OP.max))

                def XBF(i):
                    R_x.wait(v, i)
                    if i >= 1:
                        v.wait_ge(S["mm2"].h, b["mm2"] + i)
                    VC["xbf", i] = C(v.tensor_copy(
                        out=_flat(a_xbf, 0, NCH * TT, XBROW),
                        in_=xf(i)))

                def TAIL(i):
                    k = i % 2
                    pa, pq = pb[2 * k], pb[2 * k + 1]
                    v.wait_ge(S["mm2"].h, b["mm2"] + i + 1)

                    C(v.tensor_scalar(stt(k, 0), pa[:, 0:TT], 1.0 / D, None,
                                      OP.mult))
                    C(v.tensor_scalar(stt(k, 1), pq[:, 0:TT], 1.0 / D, None,
                                      OP.mult))
                    C(v.tensor_mul(stt(k, 3), stt(k, 0), stt(k, 0)))
                    VC["var", i] = C(v.tensor_sub(stt(k, 2), stt(k, 1),
                                                  stt(k, 3)))
                    v.wait_ge(S["sd"].h, b["sd"] + i + 1)
                    C(v.reciprocal(out=_flat(a_rstd, 0, TT, RROW),
                                   in_=stt(k, 3)))
                    if i >= 2:
                        R_lnsp.wait(v, i - 2)
                    lt = bass.AP(a_lnh, k * NCH * TT,
                                 [[2 * NCH * TT, 128], [TT, NCH], [1, TT]])
                    C(v.tensor_tensor(out=lt, in0=x3(i), in1=stt_b(k, 0),
                                      op=OP.subtract))
                    C(v.tensor_tensor(out=lt, in0=lt, in1=rstd_b(k),
                                      op=OP.mult))
                    C(v.reduce_max(out=tmax[:],
                                   in_=_flat(a_lnh, k * NCH * TT, NCH * TT,
                                             2 * NCH * TT),
                                   axis=AX.X, apply_absolute_value=True))
                    VC["ln", i] = C(v.tensor_tensor(out=rmax[:], in0=rmax[:],
                                                    in1=tmax[:], op=OP.max))

                def WQ(c):
                    v.wait_ge(S["wact"].h, b["wact"] + 2 * (c + 1))
                    fc, q = c // 4, c % 4
                    VC["wq", c] = C(v.tensor_copy(
                        out=_flat(w1q_sb, fc * H + q * 1024, 1024, NCH * H),
                        in_=wqs(c)))

                wq_done = 0
                for i in range(NT):
                    XBF(i)
                    if i >= 1:
                        TAIL(i - 1)
                    if i == 2:
                        n1 = fin_scalar_v(v, b["g"] + 48,
                                          row256[0:1, 0:128])
                        VC["s1w"] = n1
                        v.wait_ge(S["g"].h, b["g"] + 80)
                        inv_n["w1"] = C(v.reciprocal(out=invw1[:],
                                                     in_=sw1_b[:]))
                        n2 = fin_scalar_v(v, b["g"] + 48,
                                          row256[0:1, 128:256])
                        VC["s2w"] = n2
                        v.wait_ge(S["g"].h, b["g"] + 112)
                        inv_n["w2"] = C(v.reciprocal(out=invw2[:],
                                                     in_=sw2_b[:]))
                    if i >= 3:
                        for _ in range(4):
                            WQ(wq_done)
                            wq_done += 1
                TAIL(NT - 1)
                while wq_done < 32:
                    WQ(wq_done)
                    wq_done += 1
                VC["wq_last"] = VC["wq", 31]

            @blk.sync
            def _(s):
                nsp = 0
                for i in range(NT):
                    if i >= 3:
                        s.wait_ge(VCH.h, VC["ln", i - 3])
                        s.wait_ge(S["sq"].h, b["sq"] + i - 2)
                    R_x.dma(s, x3(i), xT_r[:, :, i * TT:(i + 1) * TT])
                    if i >= 3:
                        j = i - 3
                        s.wait_ge(VCH.h, VC["ln", j])
                        dst = ln_d[:, :, j * TT:(j + 1) * TT].transpose(
                            [1, 0, 2])
                        R_lnsp.dma(s, dst,
                                   bass.AP(a_lnh, (j % 2) * NCH * TT,
                                           [[2 * NCH * TT, 128], [TT, NCH],
                                            [1, TT]]))
                        nsp += 1
                for j in range(nsp, NT):
                    s.wait_ge(VCH.h, VC["ln", j])
                    dst = ln_d[:, :, j * TT:(j + 1) * TT].transpose([1, 0, 2])
                    R_lnsp.dma(s, dst,
                               bass.AP(a_lnh, (j % 2) * NCH * TT,
                                       [[2 * NCH * TT, 128], [TT, NCH],
                                        [1, TT]]))

            @blk.scalar
            def _(sc):
                def SQ(i):
                    R_x.wait(sc, i)
                    if i >= 2:
                        sc.wait_ge(S["mm2"].h, b["mm2"] + i - 1)
                    S["sq"].inc(sc.activation(
                        out=_flat(a_sqbf, (i % 2) * NCH * TT, NCH * TT,
                                  2 * NCH * TT),
                        in_=xf(i), func=AF.Square))

                def SD(i):
                    sc.wait_ge(VCH.h, VC["var", i])
                    S["sd"].inc(sc.activation(
                        out=stt(i % 2, 3), in_=stt(i % 2, 2),
                        func=AF.Sqrt, bias=EPS_A))

                def WQA1(c):
                    R_w1f.wait(sc, 4 + c)
                    sc.wait_ge(VCH.h, inv_n["w1"])
                    S["wact"].inc(sc.activation(
                        out=wsl(c), in_=wsl(c), func=AF.Identity,
                        scale=invw1[:, 0:1], bias=CP_A))
                    if c >= 4:
                        sc.wait_ge(VCH.h, VC["wq", c - 4])
                    i2 = sc.activation(out=wqs(c), in_=wsl(c),
                                       func=AF.Identity, bias=CN_A)
                    i2.wait_op(S["wact"].h, S["wact"].n, "sem-ge")
                    S["wact"].inc(i2)

                wq_a = 0
                for i in range(NT):
                    SQ(i)
                    if i >= 1:
                        SD(i - 1)
                    if i >= 2:
                        for _ in range(4):
                            WQA1(wq_a)
                            wq_a += 1
                SD(NT - 1)
                while wq_a < 32:
                    WQA1(wq_a)
                    wq_a += 1

            @blk.tensor
            def _(t):
                for i in range(NT):
                    t.wait_ge(VCH.h, VC["xbf", i])
                    t.wait_ge(S["sq"].h, b["sq"] + i + 1)
                    if i >= 2:
                        t.wait_ge(VCH.h, VC["var", i - 2])
                    pa, pq = pb[2 * (i % 2)], pb[2 * (i % 2) + 1]
                    for c in range(NCH):
                        t.matmul(pa[:, 0:TT], ones_bf[:],
                                 _flat(a_xbf, c * TT, TT, XBROW),
                                 start=(c == 0), stop=(c == NCH - 1))
                    for c in range(NCH):
                        ins = t.matmul(pq[:, 0:TT], ones_bf[:],
                                       _flat(a_sqbf, (i % 2) * NCH * TT
                                             + c * TT, TT, 2 * NCH * TT),
                                       start=(c == 0), stop=(c == NCH - 1))
                    S["mm2"].inc(ins)

            @blk.gpsimd
            def _(g):
                g.wait_ge(S["ld"].h, S["ld"].n)
                for q in range(4):
                    if q >= 2:
                        g.wait_ge(VCH.h, VC["wm", q - 2])
                    R_w1f.dma(g, wshl(q), wsh[:, q * 2048:(q + 1) * 2048])
                g.wait_ge(VCH.h, VC["wm", 3])
                DMA(g, S["g"], ccw_in[0:1, 0:128].transpose([1, 0]),
                    rmw1[:, 0:1])
                DMA(g, S["g"], ccw_in[0:1, 128:256].transpose([1, 0]),
                    rmw2[:, 0:1])
                g.wait_ge(S["g"].h, b["g"] + 32)
                S["cc"].inc(g.collective_compute(
                    "AllReduce", OP.max, replica_groups=[list(range(8))],
                    ins=[ccw_in], outs=[ccw_out]))
                g.wait_ge(S["cc"].h, 1)
                DMA(g, S["g"], row256[:], ccw_out)
                fin_bcast_g(g, VC["s1w"], 0, sw1_b[:])
                fin_bcast_g(g, VC["s2w"], 1, sw2_b[:])
                for c in range(24):
                    if c >= 2:
                        g.wait_ge(S["wact"].h, b["wact"] + 2 * (c - 1))
                    fc, q = c // 4, c % 4
                    R_w1f.dma(g, wsl(c),
                              w1T_r[:, fc, q * 1024:q * 1024 + 1024])
                g.wait_ge(VCH.h, VC["ln", NT - 1])
                DMA(g, S["sp"], cc_in[0:1, :].transpose([1, 0]), rmax[:, 0:1])
                g.wait_ge(S["sp"].h, S["sp"].n)
                S["cc"].inc(g.collective_compute(
                    "AllReduce", OP.max, replica_groups=[list(range(8))],
                    ins=[cc_in], outs=[cc_out]))
                for c in range(24, 32):
                    g.wait_ge(S["wact"].h, b["wact"] + 2 * (c - 1))
                    fc, q = c // 4, c % 4
                    R_w1f.dma(g, wsl(c),
                              w1T_r[:, fc, q * 1024:q * 1024 + 1024])
        return VC["wq_last"]

    def emit_passB(ncc, col, w1q_ap, first, guard, ln_wait_fn,
                   w1q_vc, w1q_sb=None):
        b = snap()
        sfx = "_%d" % ncc
        lb_base = R_lb.k
        hsp_base = R_hsp.k
        es = contextlib.ExitStack()
        lbuf = es.enter_context(
            nc.sbuf_tensor("lbuf" + sfx, [128, 2, NCH, TT], F16))
        qf = es.enter_context(
            nc.sbuf_tensor("qf" + sfx, [128, NCH, TT], F32))
        qib = es.enter_context(
            nc.sbuf_tensor("qib" + sfx, [128, NCH, TT], I32))
        qbf = es.enter_context(
            nc.sbuf_tensor("qbf" + sfx, [128, 2, NCH, TT], BF16))
        gbuf = es.enter_context(
            nc.sbuf_tensor("gbuf" + sfx, [128, 2, 4, TT], F16))
        if first:
            wfstg = es.enter_context(
                nc.sbuf_tensor("wfstg" + sfx, [128, 2, 4096], F32))
            wfbf = es.enter_context(
                nc.sbuf_tensor("wfbf" + sfx, [128, 2, 4096], BF16))
            w2st = es.enter_context(
                nc.sbuf_tensor("w2st" + sfx, [128, 2, 1024], F32))
            wqi2 = es.enter_context(
                nc.sbuf_tensor("wqi2" + sfx, [128, 2, 1024], I32))
            w2qb = es.enter_context(
                nc.sbuf_tensor("w2qb" + sfx, [128, 2, 1024], BF16))
        with nc.Block("b_ffnB" + str(ncc)) as blk:
            LROW = 2 * NCH * TT
            GROW = 2 * 4 * TT
            VC = {}

            @blk.vector
            def _(v):
                n_s = fin_scalar_v(v, b["g"] + 16, row128[:])
                VC["s"] = n_s
                v.wait_ge(S["g"].h, b["g"] + 48)
                C(v.reciprocal(out=scl[:, col + 1:col + 2],
                               in_=scl[:, col:col + 1]))
                C(v.tensor_mul(scl[:, col + 2:col + 3], sw1_b[:, 0:1],
                               scl[:, col:col + 1]))
                VC["fin"] = C(v.tensor_scalar(b1s[:], b1sb[:],
                                              scl[:, col + 2:col + 3], None,
                                              OP.mult))

                def QNT(i):
                    R_lb.wait(v, lb_base + i)
                    if i >= 2:
                        v.wait_ge(S["mm"].h, b["mm"] + HCH * (i - 1))
                    lf = _flat(lbuf, (i % 2) * NCH * TT, NCH * TT, LROW)
                    C(v.tensor_scalar(_flat(qf, 0, NCH * TT, NCH * TT), lf,
                                      scl[:, col + 1:col + 2], CMAGIC,
                                      OP.mult, OP.add))
                    C(v.tensor_scalar(_flat(qib, 0, NCH * TT, NCH * TT),
                                      _flat(qf, 0, NCH * TT, NCH * TT),
                                      CMAGIC, None, OP.subtract))
                    VC["qa", i] = C(v.tensor_copy(
                        out=_flat(qbf, (i % 2) * NCH * TT, NCH * TT, LROW),
                        in_=_flat(qib, 0, NCH * TT, NCH * TT)))

                def HMAX(gidx):
                    v.wait_ge(S["ev"].h, b["ev"] + 4 * (gidx + 1))
                    C(v.reduce_max(out=tmax[:],
                                   in_=_flat(gbuf, (gidx % 2) * 4 * TT,
                                             4 * TT, GROW),
                                   axis=AX.X, apply_absolute_value=True))
                    VC["hm", gidx] = C(v.tensor_tensor(
                        out=rmax[:], in0=rmax[:], in1=tmax[:], op=OP.max))

                def W2CP(a):
                    v.wait_ge(S["w2act"].h, b["w2act"] + 2 * (a + 1))
                    if a >= 2:
                        R_w2qsp.wait(v, a - 2)
                    VC["w2q", a] = C(v.tensor_copy(
                        out=_flat(w2qb, (a % 2) * 1024, 1024, 2048),
                        in_=_flat(wqi2, (a % 2) * 1024, 1024, 2048)))

                QNT(0)
                if NT > 1:
                    QNT(1)
                C(v.memset(rmax[:], 0.0))
                w2done = 0
                for gidx in range(8 * NT):
                    HMAX(gidx)
                    if (first and w2done < HCH and gidx >= 3
                            and gidx % 2 == 1):
                        W2CP(w2done)
                        w2done += 1
                    if gidx % 8 == 7 and gidx // 8 + 2 < NT:
                        QNT(gidx // 8 + 2)
                while first and w2done < HCH:
                    W2CP(w2done)
                    w2done += 1
                if first:
                    for t_ in range(10):
                        R_wfl.wait(v, t_)
                        if t_ >= 2:
                            R_wfsp.wait(v, t_ - 2)
                        VC["wcv", t_] = C(v.tensor_copy(
                            out=_flat(wfbf, (t_ % 2) * 4096, 4096, 2 * 4096),
                            in_=_flat(wfstg, (t_ % 2) * 4096, 4096,
                                      2 * 4096)))

            @blk.sync
            def _(s):
                barrier(s, guard)

                def LB(i):
                    if i >= 2:
                        s.wait_ge(VCH.h, VC["qa", i - 2])
                    ln_wait_fn(s, i)
                    src2 = ln_d[:, :, i * TT:(i + 1) * TT].transpose(
                        [1, 0, 2])
                    R_lb.dma(s,
                             bass.AP(lbuf, (i % 2) * NCH * TT,
                                     [[LROW, 128], [TT, NCH], [1, TT]]),
                             src2)

                LB(0)
                if NT > 1:
                    LB(1)
                for i in range(NT):
                    if first:
                        for a in range(4 * i, 4 * i + 4):
                            if a >= 2:
                                s.wait_ge(S["w2act"].h,
                                          b["w2act"] + 2 * (a - 1))
                            R_w2f.dma(s, _flat(w2st, (a % 2) * 1024, 1024,
                                               2048),
                                      w2T_r[:, a, :])
                    if i + 2 < NT:
                        LB(i + 2)

            @blk.tensor
            def _(t):
                t.wait_ge(VCH.h, VC["fin"])
                if w1q_vc:
                    t.wait_ge(VCH.h, w1q_vc)
                for i in range(NT):
                    t.wait_ge(VCH.h, VC["qa", i])
                    for j in range(HCH):
                        m = HCH * i + j
                        if m >= 2:
                            t.wait_ge(S["ev"].h, b["ev"] + m - 1)
                        if not first:
                            R_w1ql.wait(t, j)
                        pj = pb[j % 2]
                        for c in range(NCH):
                            ins = t.matmul(
                                pj[:, 0:TT],
                                w1q_ap(c, j),
                                _flat(qbf, (i % 2) * NCH * TT + c * TT, TT,
                                      LROW),
                                start=(c == 0), stop=(c == NCH - 1))
                        S["mm"].inc(ins)

            @blk.scalar
            def _(sc):
                def W2QA(a):
                    R_w2f.wait(sc, a)
                    sc.wait_ge(VCH.h, inv_n["w2"])
                    S["w2act"].inc(sc.activation(
                        out=_flat(w2st, (a % 2) * 1024, 1024, 2048),
                        in_=_flat(w2st, (a % 2) * 1024, 1024, 2048),
                        func=AF.Identity, scale=invw2[:, 0:1], bias=CP_A))
                    if a >= 2:
                        sc.wait_ge(VCH.h, VC["w2q", a - 2])
                    i2 = sc.activation(
                        out=_flat(wqi2, (a % 2) * 1024, 1024, 2048),
                        in_=_flat(w2st, (a % 2) * 1024, 1024, 2048),
                        func=AF.Identity, bias=CN_A)
                    i2.wait_op(S["w2act"].h, S["w2act"].n, "sem-ge")
                    S["w2act"].inc(i2)

                w2a = 0
                for i in range(NT):
                    for j in range(HCH):
                        m = HCH * i + j
                        sc.wait_ge(S["mm"].h, b["mm"] + m + 1)
                        gidx = m // 4
                        if m % 4 == 0 and gidx >= 2:
                            R_hsp.wait(sc, hsp_base + gidx - 2)
                            sc.wait_ge(VCH.h, VC["hm", gidx - 2])
                        S["ev"].inc(sc.activation(
                            out=_flat(gbuf, (gidx % 2) * 4 * TT
                                      + (m % 4) * TT, TT, GROW),
                            in_=pb[j % 2][:, 0:TT], func=AF.Gelu,
                            scale=scl[:, col + 2:col + 3],
                            bias=b1s[:, j:j + 1]))
                        if first and j % 8 == 7 and w2a < HCH:
                            W2QA(w2a)
                            w2a += 1
                while first and w2a < HCH:
                    W2QA(w2a)
                    w2a += 1

            @blk.gpsimd
            def _(g):
                g.wait_ge(S["cc"].h, ncc)
                DMA(g, S["g"], row128[:], cc_out[0:1, :])
                fin_bcast_g(g, VC["s"], 3 + ncc, scl[:, col:col + 1])
                if first:
                    g.wait_ge(VCH.h, w1q_vc)
                    for j in range(HCH):
                        src = bass.AP(w1q_sb, j * 128,
                                      [[NCH * H, 128], [H, NCH], [1, 128]])
                        R_w1qsp.dma(g, w1qd_ap(j), src)
                w2sp = 0
                for gidx in range(8 * NT):
                    i, gg = gidx // 8, gidx % 8
                    g.wait_ge(S["ev"].h, b["ev"] + 4 * (gidx + 1))
                    dst = h_d[i, 4 * gg:4 * gg + 4].transpose([1, 0, 2])
                    R_hsp.dma(g, dst,
                              bass.AP(gbuf, (gidx % 2) * 4 * TT,
                                      [[GROW, 128], [TT, 4], [1, TT]]))
                    if (first and w2sp < HCH and gidx >= 4
                            and gidx % 2 == 0):
                        a = w2sp
                        g.wait_ge(VCH.h, VC["w2q", a])
                        dst2 = bass.AP(w2q_t, a * 128,
                                       [[HCH * 128, 128],
                                        [128 * HCH * 128, NCH], [1, 128]])
                        src2 = bass.AP(w2qb, (a % 2) * 1024,
                                       [[2048, 128], [128, NCH], [1, 128]])
                        R_w2qsp.dma(g, dst2, src2)
                        w2sp += 1
                while first and w2sp < HCH:
                    a = w2sp
                    g.wait_ge(VCH.h, VC["w2q", a])
                    dst2 = bass.AP(w2q_t, a * 128,
                                   [[HCH * 128, 128],
                                    [128 * HCH * 128, NCH], [1, 128]])
                    src2 = bass.AP(w2qb, (a % 2) * 1024,
                                   [[2048, 128], [128, NCH], [1, 128]])
                    R_w2qsp.dma(g, dst2, src2)
                    w2sp += 1
                g.wait_ge(VCH.h, VC["hm", 8 * NT - 1])
                DMA(g, S["sp"], cc_in[0:1, :].transpose([1, 0]), rmax[:, 0:1])
                g.wait_ge(S["sp"].h, S["sp"].n)
                S["cc"].inc(g.collective_compute(
                    "AllReduce", OP.max, replica_groups=[list(range(8))],
                    ins=[cc_in], outs=[cc_out]))
                if first:
                    awsrc = [winT_r, waT_r, wbT_r, wvT_r, woT_r]

                    def WFL(t_):
                        if t_ >= 2:
                            g.wait_ge(VCH.h, VC["wcv", t_ - 2])
                        w_, hh_ = t_ // 2, t_ % 2
                        src = awsrc[w_][:, 4 * hh_:4 * hh_ + 4, :]
                        R_wfl.dma(g, bass.AP(wfstg, (t_ % 2) * 4096,
                                             [[2 * 4096, 128], [D, 4],
                                              [1, D]]),
                                  src)

                    WFL(0)
                    WFL(1)
                    for t_ in range(10):
                        g.wait_ge(VCH.h, VC["wcv", t_])
                        w_, hh_ = t_ // 2, t_ % 2
                        dst = bass.AP(
                            watt_t, w_ * 128 * 8192 + hh_ * 4096,
                            [[8192, 128], [1, 4096]])
                        R_wfsp.dma(g, dst,
                                   _flat(wfbf, (t_ % 2) * 4096, 4096,
                                         2 * 4096))
                        if t_ + 2 < 10:
                            WFL(t_ + 2)
        es.close()

    def emit_passC(ncc, col, res_src_fn, dst_fn, x0dt, outdt, x0_wait_fn,
                  guard, hsp_base):
        b = snap()
        sfx = "_%d" % ncc
        h_base = R_h.k
        w2b_base = R_w2b.k
        x0_base = R_x0.k
        sp_base = R_x1sp.k
        es = contextlib.ExitStack()
        hbuf = es.enter_context(
            nc.sbuf_tensor("hbuf" + sfx, [128, 3, NCH, TT], F16))
        hqf = es.enter_context(
            nc.sbuf_tensor("hqf" + sfx, [128, NCH, TT], F32))
        hqi = es.enter_context(
            nc.sbuf_tensor("hqi" + sfx, [128, NCH, TT], I32))
        hqbf = es.enter_context(
            nc.sbuf_tensor("hqbf" + sfx, [128, 2, HCH, TT], BF16))
        w2b = es.enter_context(
            nc.sbuf_tensor("w2b" + sfx, [128, 3, HCH, 128], BF16))
        x0b = es.enter_context(
            nc.sbuf_tensor("x0b" + sfx, [128, 2, NCH, TT], x0dt))
        obuf = es.enter_context(
            nc.sbuf_tensor("obuf" + sfx, [128, 4, TT], F32))
        x1b = es.enter_context(
            nc.sbuf_tensor("x1b" + sfx, [128, 2, NCH, TT], outdt))
        with nc.Block("b_ffnC" + str(ncc)) as blk:
            HROW = 3 * NCH * TT
            QROW = 2 * HCH * TT
            WROW = 3 * HCH * 128
            XROW = 2 * NCH * TT
            OROW = 4 * TT
            VC = {}

            @blk.vector
            def _(v):
                n_s = fin_scalar_v(v, b["g"] + 16, row128[:])
                VC["s"] = n_s
                v.wait_ge(S["g"].h, b["g"] + 48)
                C(v.reciprocal(out=scl[:, col + 1:col + 2],
                               in_=scl[:, col:col + 1]))
                C(v.tensor_mul(scl[:, col + 2:col + 3], sw2_b[:, 0:1],
                               scl[:, col:col + 1]))
                C(v.tensor_scalar(scl[:, col + 2:col + 3],
                                  scl[:, col + 2:col + 3], 0.5, None,
                                  OP.mult))
                VC["fin"] = C(v.tensor_scalar(b2s[:], b2sb[:],
                                              scl[:, col + 2:col + 3], None,
                                              OP.mult))

                def QNTH(i, gq):
                    m = 4 * i + gq
                    R_h.wait(v, h_base + m)
                    if gq == 0 and i >= 2:
                        v.wait_ge(S["mm"].h, b["mm"] + NCH * (i - 1))
                    hf = _flat(hbuf, (m % 3) * NCH * TT, NCH * TT, HROW)
                    C(v.tensor_scalar(_flat(hqf, 0, NCH * TT, NCH * TT), hf,
                                      scl[:, col + 1:col + 2], CMAGIC,
                                      OP.mult, OP.add))
                    C(v.tensor_scalar(_flat(hqi, 0, NCH * TT, NCH * TT),
                                      _flat(hqf, 0, NCH * TT, NCH * TT),
                                      CMAGIC, None, OP.subtract))
                    VC["qa", m] = C(v.tensor_copy(
                        out=_flat(hqbf, (i % 2) * HCH * TT + gq * NCH * TT,
                                  NCH * TT, QROW),
                        in_=_flat(hqi, 0, NCH * TT, NCH * TT)))

                def RES(i):
                    for d in range(NCH):
                        m = NCH * i + d
                        v.wait_ge(S["ev"].h, b["ev"] + m + 1)
                        R_x0.wait(v, x0_base + i)
                        if d == 0 and i >= 2:
                            R_x1sp.wait(v, sp_base + i - 2)
                        VC["res", m] = C(v.tensor_add(
                            _flat(x1b, (i % 2) * NCH * TT + d * TT, TT, XROW),
                            _flat(obuf, (d % 4) * TT, TT, OROW),
                            _flat(x0b, (i % 2) * NCH * TT + d * TT, TT,
                                  XROW)))

                for i in range(NT):
                    for gq in range(4):
                        QNTH(i, gq)
                    if i >= 1:
                        RES(i - 1)
                RES(NT - 1)

            @blk.sync
            def _(s):
                barrier(s, guard)
                for i in range(NT):
                    for gq in range(4):
                        m = 4 * i + gq
                        if m >= 3:
                            s.wait_ge(VCH.h, VC["qa", m - 3])
                        R_hsp.wait(s, hsp_base + 8 * i + 2 * gq + 1)
                        src = h_d[i, 8 * gq:8 * gq + 8].transpose([1, 0, 2])
                        R_h.dma(s,
                                bass.AP(hbuf, (m % 3) * NCH * TT,
                                        [[HROW, 128], [TT, NCH], [1, TT]]),
                                src)

            @blk.scalar
            def _(sc):
                def W2S(m):
                    d = m % NCH
                    if m == 0:
                        barrier(sc, guard)
                    if m >= 3:
                        sc.wait_ge(S["mm"].h, b["mm"] + m - 2)
                    R_w2b.dma(sc,
                              bass.AP(w2b, (m % 3) * HCH * 128,
                                      [[WROW, 128], [128, HCH], [1, 128]]),
                              w2q_d[d])

                def OEV(m):
                    d = m % NCH
                    sc.wait_ge(S["mm"].h, b["mm"] + m + 1)
                    if m >= 4:
                        sc.wait_ge(VCH.h, VC["res", m - 4])
                    S["ev"].inc(sc.activation(
                        out=_flat(obuf, (d % 4) * TT, TT, OROW),
                        in_=pb[d % 4][:, 0:TT], func=AF.Identity,
                        scale=scl[:, col + 2:col + 3],
                        bias=b2s[:, d:d + 1]))

                for m in range(3):
                    W2S(m)
                for m in range(NCH * NT):
                    OEV(m)
                    if m + 3 < NCH * NT:
                        W2S(m + 3)

            @blk.tensor
            def _(t):
                t.wait_ge(VCH.h, VC["fin"])
                for i in range(NT):
                    t.wait_ge(VCH.h, VC["qa", 4 * i + 3])
                    for d in range(NCH):
                        m = NCH * i + d
                        R_w2b.wait(t, w2b_base + m)
                        if m >= 4:
                            t.wait_ge(S["ev"].h, b["ev"] + m - 3)
                        pB = pb[d % 4]
                        for j in range(HCH):
                            ins = t.matmul(
                                pB[:, 0:TT],
                                _flat(w2b, (m % 3) * HCH * 128 + j * 128,
                                      128, WROW),
                                _flat(hqbf, (i % 2) * HCH * TT + j * TT,
                                      TT, QROW),
                                start=(j == 0), stop=(j == HCH - 1))
                        S["mm"].inc(ins)

            @blk.gpsimd
            def _(g):
                g.wait_ge(S["cc"].h, ncc)
                DMA(g, S["g"], row128[:], cc_out[0:1, :])
                fin_bcast_g(g, VC["s"], 8 + ncc, scl[:, col:col + 1])
                barrier(g, guard)
                for i in range(NT):
                    x0_wait_fn(g, i)
                    if i >= 2:
                        g.wait_ge(VCH.h, VC["res", NCH * (i - 1) - 1])
                    R_x0.dma(g,
                             bass.AP(x0b, (i % 2) * NCH * TT,
                                     [[XROW, 128], [TT, NCH], [1, TT]]),
                             res_src_fn(i))
                    if i >= 1:
                        j = i - 1
                        g.wait_ge(VCH.h, VC["res", NCH * (j + 1) - 1])
                        R_x1sp.dma(g, dst_fn(j),
                                   bass.AP(x1b, (j % 2) * NCH * TT,
                                           [[XROW, 128], [TT, NCH],
                                            [1, TT]]))
                j = NT - 1
                g.wait_ge(VCH.h, VC["res", NCH * (j + 1) - 1])
                R_x1sp.dma(g, dst_fn(j),
                           bass.AP(x1b, (j % 2) * NCH * TT,
                                   [[XROW, 128], [TT, NCH], [1, TT]]))
                R_x1sp.wait_all(g)
        es.close()

    def emit_attn(guard, x1sp_base):
        with (
            nc.sbuf_tensor("win_sb", [128, NCH, D], BF16) as win_sb,
            nc.sbuf_tensor("wa_sb", [128, NCH, D], BF16) as wa_sb,
            nc.sbuf_tensor("wb_sb", [128, NCH, D], BF16) as wb_sb,
            nc.sbuf_tensor("wv_sb", [128, NCH, D], BF16) as wv_sb,
            nc.sbuf_tensor("wo_sb", [128, NCH, D], BF16) as wo_sb,
        ):
            wdsts = [win_sb, wa_sb, wb_sb, wv_sb, wo_sb]
            wal_base = R_wal.k
            b = snap()
            es = contextlib.ExitStack()

            def lsb(name, shape, dt):
                return es.enter_context(nc.sbuf_tensor(name, shape, dt))

            xa = lsb("xa", [128, 3, NCH, TA], F16)
            xabf = lsb("xabf", [128, 1, NCH, TA], BF16)
            sqbf = lsb("sqbf", [128, 1, NCH, TA], BF16)
            stb = lsb("stb", [128, 1, 4, TA], F32)
            rstd = lsb("rstd", [128, 1, TA], F32)
            lnf = lsb("lnf", [128, NCH, TA], F32)
            ln2bf = lsb("ln2bf", [128, 2, NCH, TA], BF16)
            xpbf = lsb("xpbf", [128, 2, NCH, TA], BF16)
            af32 = lsb("af32", [128, 8, TA], F32)
            bf32 = lsb("bf32", [128, 8, TA], F32)
            vf32 = lsb("vf32", [128, 8, TA], F32)
            bvf = lsb("bvf", [128, 1, TA], F32)
            hb = lsb("hb", [128, 2, NCH, TA], F32)
            sg2 = lsb("sg2", [128, 4, TA], F32)
            cellbf = lsb("cellbf", [128, 2, NCH, TA], BF16)
            x2b = lsb("x2b", [128, 1, NCH, TA], F16)
            xbf2 = lsb("xbf2", [128, 1, NCH, TA], BF16)
            sqbf2 = lsb("sqbf2", [128, 1, NCH, TA], BF16)
            stb3 = lsb("stb3", [128, 1, 4, TA], F32)
            rstd3 = lsb("rstd3", [128, 1, TA], F32)
            ln3b = lsb("ln3b", [128, 1, NCH, TA], F16)
            with nc.Block("b6b") as blk:
                XROW = 2 * NCH * TA
                X1ROW = NCH * TA
                SROW = 4 * TA
                RROW = 2 * TA
                FROW = NCH * TA
                VC = {}

                rowlens = {id(xa): 3 * NCH * TA, id(xabf): X1ROW, id(sqbf): X1ROW,
                           id(ln2bf): XROW, id(xpbf): XROW, id(hb): XROW,
                           id(cellbf): XROW, id(x2b): X1ROW, id(xbf2): X1ROW,
                           id(sqbf2): X1ROW, id(ln3b): X1ROW, id(lnf): FROW}

                def f3(th, k):
                    return bass.AP(th, k * NCH * TA,
                                   [[rowlens[id(th)], 128], [TA, NCH],
                                    [1, TA]])

                def ff(th, k, n=NCH * TA, extra=0):
                    return _flat(th, k * NCH * TA + extra, n,
                                 rowlens[id(th)])

                def stslot(th, slot):
                    return _flat(th, slot * TA, TA, SROW)

                def stslot_b(th, slot):
                    return bass.AP(th, slot * TA,
                                   [[SROW, 128], [0, NCH], [1, TA]])

                def rstd_bc(th):
                    return bass.AP(th, 0, [[TA, 128], [0, NCH], [1, TA]])

                def half(th, k):
                    return _flat(th, k * TA, TA, RROW)

                def half4(th, k):
                    return _flat(th, k * TA, TA, 4 * TA)

                def half8(th, k):
                    return _flat(th, k * TA, TA, 8 * TA)

                def psl(bank, j):
                    return pb[bank][:, j * TA:(j + 1) * TA]

                lnf3 = bass.AP(lnf, 0, [[FROW, 128], [TA, NCH], [1, TA]])

                @blk.vector
                def _(v):
                    C(v.memset(rmax[:], 0.0))

                    def PRE(k):
                        R_xa.wait(v, k)
                        if k >= 1:
                            v.wait_ge(S["mm2"].h, b["mm2"] + k)
                        VC["pre", k] = C(v.tensor_copy(out=ff(xabf, 0),
                                                       in_=ff(xa, k % 3)))

                    def TAILS(k):
                        v.wait_ge(S["mm2"].h, b["mm2"] + k + 1)
                        C(v.tensor_scalar(stslot(stb, 0), psl(6, 0),
                                          1.0 / D, None, OP.mult))
                        C(v.tensor_scalar(stslot(stb, 1), psl(6, 1),
                                          1.0 / D, None, OP.mult))
                        C(v.tensor_mul(stslot(stb, 3), stslot(stb, 0),
                                       stslot(stb, 0)))
                        VC["var", k] = C(v.tensor_sub(stslot(stb, 2),
                                                      stslot(stb, 1),
                                                      stslot(stb, 3)))
                        v.wait_ge(S["sd"].h, b["sd"] + k + 1)
                        C(v.reciprocal(out=_flat(rstd, 0, TA, TA),
                                       in_=stslot(stb, 3)))
                        if k >= 2:
                            v.wait_ge(S["mi"].h, b["mi"] + NCH * (k - 1))
                        C(v.tensor_tensor(out=lnf3, in0=f3(xa, k % 3),
                                          in1=stslot_b(stb, 0),
                                          op=OP.subtract))
                        VC["qa", k] = C(v.tensor_tensor(
                            out=f3(ln2bf, k % 2), in0=lnf3,
                            in1=rstd_bc(rstd), op=OP.mult))

                    PRE(0)
                    TAILS(0)
                    if NA > 1:
                        PRE(1)
                    for k in range(NA):
                        if k + 1 < NA:
                            TAILS(k + 1)
                        if k + 2 < NA:
                            PRE(k + 2)
                        for f in range(NCH):
                            mf = NCH * k + f
                            v.wait_ge(S["bee"].h, b["bee"] + mf + 1)
                            v.wait_ge(S["vee"].h, b["vee"] + mf + 1)
                            VC["bv", mf] = C(v.tensor_mul(
                                _flat(bvf, 0, TA, TA), half8(bf32, f),
                                half8(vf32, f)))
                            v.wait_ge(S["ale"].h, b["ale"] + mf + 1)
                            init = (0.0 if k == 0 else
                                    bass.AP(hb, (1 - k % 2) * NCH * TA
                                            + f * TA + TA - 1,
                                            [[XROW, 128], [1, 1]]))
                            VC["scan", mf] = C(v.tensor_tensor_scan(
                                ff(hb, k % 2, TA, f * TA),
                                half8(af32, f),
                                _flat(bvf, 0, TA, TA), init,
                                OP.mult, OP.add))
                        for f in range(NCH):
                            mf = NCH * k + f
                            v.wait_ge(S["sg"].h, b["sg"] + mf + 1)
                            if f == 0 and k >= 2:
                                v.wait_ge(S["mo"].h, b["mo"] + NCH * (k - 1))
                            VC["cell", mf] = C(v.tensor_mul(
                                ff(cellbf, k % 2, TA, f * TA),
                                ff(hb, k % 2, TA, f * TA),
                                half4(sg2, f % 4)))
                        for d in range(NCH):
                            md = NCH * k + d
                            v.wait_ge(S["mo"].h, b["mo"] + md + 1)
                            if d == 0 and k >= 1:
                                R_x2sp.wait(v, k - 1)
                            VC["x2", md] = C(v.tensor_add(
                                ff(x2b, 0, TA, d * TA), pb[1][:, 0:TA],
                                ff(xa, k % 3, TA, d * TA)))
                        if k >= 1:
                            v.wait_ge(S["mm3"].h, b["mm3"] + k)
                        VC["a3", k] = C(v.tensor_copy(out=ff(xbf2, 0),
                                                      in_=ff(x2b, 0)))
                        VC["sq2v", k] = C(v.tensor_mul(
                            ff(sqbf2, 0), ff(x2b, 0), ff(x2b, 0)))
                        v.wait_ge(S["mm3"].h, b["mm3"] + k + 1)
                        C(v.tensor_scalar(stslot(stb3, 0), psl(7, 0),
                                          1.0 / D, None, OP.mult))
                        C(v.tensor_scalar(stslot(stb3, 1), psl(7, 1),
                                          1.0 / D, None, OP.mult))
                        C(v.tensor_mul(stslot(stb3, 3), stslot(stb3, 0),
                                       stslot(stb3, 0)))
                        VC["var3", k] = C(v.tensor_sub(stslot(stb3, 2),
                                                       stslot(stb3, 1),
                                                       stslot(stb3, 3)))
                        v.wait_ge(S["sd3"].h, b["sd3"] + k + 1)
                        C(v.reciprocal(out=_flat(rstd3, 0, TA, TA),
                                       in_=stslot(stb3, 3)))
                        if k >= 1:
                            R_ln3sp.wait(v, k - 1)
                        C(v.tensor_tensor(out=lnf3, in0=f3(x2b, 0),
                                          in1=stslot_b(stb3, 0),
                                          op=OP.subtract))
                        C(v.tensor_tensor(
                            out=f3(ln3b, 0), in0=lnf3,
                            in1=rstd_bc(rstd3), op=OP.mult))
                        C(v.reduce_max(out=tmax[:], in_=ff(ln3b, 0),
                                       axis=AX.X,
                                       apply_absolute_value=True))
                        VC["ln3", k] = C(v.tensor_tensor(
                            out=rmax[:], in0=rmax[:], in1=tmax[:],
                            op=OP.max))

                @blk.sync
                def _(s):
                    barrier(s, guard)
                    for w in range(5):
                        R_wfsp.wait(s, 2 * w + 1)
                        R_wal.dma(s, _flat(wdsts[w], 0, NCH * D, NCH * D),
                                  watt_d[w])
                    for i in range(NA):
                        if i >= 3:
                            s.wait_ge(VCH.h, VC["x2", NCH * (i - 2) - 1])
                            s.wait_ge(S["sq"].h, b["sq"] + i - 2)
                        R_x1sp.wait(s, x1sp_base + i // 2)
                        src = x1_d[:, :, i * TA:(i + 1) * TA].transpose(
                            [1, 0, 2])
                        R_xa.dma(s, f3(xa, i % 3), src)

                @blk.scalar
                def _(sc):
                    def SQA(k):
                        R_xa.wait(sc, k)
                        if k >= 1:
                            sc.wait_ge(S["mm2"].h, b["mm2"] + k)
                        S["sq"].inc(sc.activation(out=ff(sqbf, 0),
                                                  in_=ff(xa, k % 3),
                                                  func=AF.Square))

                    def SDA(k):
                        sc.wait_ge(VCH.h, VC["var", k])
                        S["sd"].inc(sc.activation(out=stslot(stb, 3),
                                                  in_=stslot(stb, 2),
                                                  func=AF.Sqrt, bias=EPS_A))

                    def SQ2(k):
                        sc.wait_ge(VCH.h, VC["var3", k])
                        S["sd3"].inc(sc.activation(out=stslot(stb3, 3),
                                                   in_=stslot(stb3, 2),
                                                   func=AF.Sqrt, bias=EPS_A))

                    SQA(0)
                    SDA(0)
                    for k in range(NA):
                        for e in range(NCH):
                            me = NCH * k + e
                            sc.wait_ge(S["mi"].h, b["mi"] + me + 1)
                            if k >= 2 and e == 0:
                                sc.wait_ge(S["pv"].h, b["pv"] + NCH * (k - 1))
                            S["xps"].inc(sc.activation(
                                out=ff(xpbf, k % 2, TA, e * TA),
                                in_=pb[2 + e % 2][:, 0:TA], func=AF.Silu))
                        if k + 1 < NA:
                            SQA(k + 1)
                        if k >= 1:
                            SQ2(k - 1)
                        if k + 1 < NA:
                            SDA(k + 1)
                        for g in range(24):
                            proj, f = g // 8, g % 8
                            mf = NCH * k + f
                            bank = (4, 5, 0)[g % 3]
                            sc.wait_ge(S["pv"].h, b["pv"] + 24 * k + g + 1)
                            if proj == 0:
                                if k >= 1:
                                    sc.wait_ge(VCH.h, VC["scan", mf - 8])
                                S["ale"].inc(sc.activation(
                                    out=half8(af32, f),
                                    in_=pb[bank][:, 0:TA],
                                    func=AF.Sigmoid, bias=bap_s[:, f:f + 1]))
                            elif proj == 1:
                                if k >= 1:
                                    sc.wait_ge(VCH.h, VC["bv", mf - 8])
                                S["bee"].inc(sc.activation(
                                    out=half8(bf32, f),
                                    in_=pb[bank][:, 0:TA],
                                    func=AF.Sigmoid, bias=bbp_s[:, f:f + 1]))
                            else:
                                if k >= 1:
                                    sc.wait_ge(VCH.h, VC["bv", mf - 8])
                                S["vee"].inc(sc.activation(
                                    out=half8(vf32, f),
                                    in_=pb[bank][:, 0:TA],
                                    func=AF.Tanh, bias=bvp_s[:, f:f + 1]))
                        for f in range(NCH):
                            mf = NCH * k + f
                            sc.wait_ge(VCH.h, VC["scan", mf])
                            if mf >= 4:
                                sc.wait_ge(VCH.h, VC["cell", mf - 4])
                            S["sg"].inc(sc.activation(
                                out=half4(sg2, f % 4),
                                in_=ff(hb, k % 2, TA, f * TA),
                                func=AF.Silu))
                    SQ2(NA - 1)

                @blk.tensor
                def _(t):
                    def STATS(k):
                        t.wait_ge(VCH.h, VC["pre", k])
                        t.wait_ge(S["sq"].h, b["sq"] + k + 1)
                        if k >= 1:
                            t.wait_ge(VCH.h, VC["var", k - 1])
                        for c in range(NCH):
                            t.matmul(psl(6, 0), ones_bf[:],
                                     ff(xabf, 0, TA, c * TA),
                                     start=(c == 0), stop=(c == NCH - 1))
                        for c in range(NCH):
                            ins = t.matmul(psl(6, 1), ones_bf[:],
                                           ff(sqbf, 0, TA, c * TA),
                                           start=(c == 0),
                                           stop=(c == NCH - 1))
                        S["mm2"].inc(ins)

                    def MI(k):
                        t.wait_ge(VCH.h, VC["qa", k])
                        for e in range(NCH):
                            me = NCH * k + e
                            if e >= 2:
                                t.wait_ge(S["xps"].h, b["xps"] + me - 1)
                            for c in range(NCH):
                                ins = t.matmul(
                                    pb[2 + e % 2][:, 0:TA],
                                    bass.AP(win_sb, c * D + e * 128,
                                            [[NCH * D, 128], [1, 128]]),
                                    ff(ln2bf, k % 2, TA, c * TA),
                                    start=(c == 0), stop=(c == NCH - 1))
                            S["mi"].inc(ins)

                    PBK = (4, 5, 0)
                    PSEM = ("ale", "bee", "vee")
                    PWSB = None

                    def PV(k):
                        t.wait_ge(S["xps"].h, b["xps"] + NCH * (k + 1))
                        wsbs = (wa_sb, wb_sb, wv_sb)
                        for g in range(24):
                            proj, f = g // 8, g % 8
                            bank = PBK[g % 3]
                            pg = g - 3
                            if pg >= 0:
                                pproj, pf = pg // 8, pg % 8
                                t.wait_ge(S[PSEM[pproj]].h,
                                          b[PSEM[pproj]] + NCH * k + pf + 1)
                            elif k >= 1:
                                pg2 = pg + 24
                                pproj, pf = pg2 // 8, pg2 % 8
                                t.wait_ge(S[PSEM[pproj]].h,
                                          b[PSEM[pproj]]
                                          + NCH * (k - 1) + pf + 1)
                            for c in range(NCH):
                                ins = t.matmul(
                                    pb[bank][:, 0:TA],
                                    bass.AP(wsbs[proj], c * D + f * 128,
                                            [[NCH * D, 128], [1, 128]]),
                                    ff(xpbf, k % 2, TA, c * TA),
                                    start=(c == 0), stop=(c == NCH - 1))
                            S["pv"].inc(ins)

                    def MO(k):
                        t.wait_ge(VCH.h, VC["cell", NCH * (k + 1) - 1])
                        for d in range(NCH):
                            md = NCH * k + d
                            if md >= 1:
                                t.wait_ge(VCH.h, VC["x2", md - 1])
                            for c in range(NCH):
                                ins = t.matmul(
                                    pb[1][:, 0:TA],
                                    bass.AP(wo_sb, c * D + d * 128,
                                            [[NCH * D, 128], [1, 128]]),
                                    ff(cellbf, k % 2, TA, c * TA),
                                    start=(c == 0), stop=(c == NCH - 1))
                            S["mo"].inc(ins)

                    def STATS3(k):
                        t.wait_ge(VCH.h, VC["sq2v", k])
                        if k >= 1:
                            t.wait_ge(VCH.h, VC["var3", k - 1])
                        for c in range(NCH):
                            t.matmul(psl(7, 0), ones_bf[:],
                                     ff(xbf2, 0, TA, c * TA),
                                     start=(c == 0), stop=(c == NCH - 1))
                        for c in range(NCH):
                            ins = t.matmul(psl(7, 1), ones_bf[:],
                                           ff(sqbf2, 0, TA, c * TA),
                                           start=(c == 0),
                                           stop=(c == NCH - 1))
                        S["mm3"].inc(ins)

                    STATS(0)
                    R_wal.wait(t, wal_base + 0)
                    MI(0)
                    for k in range(NA):
                        if k + 1 < NA:
                            STATS(k + 1)
                        if k == 0:
                            for w in range(1, 4):
                                R_wal.wait(t, wal_base + w)
                        PV(k)
                        if k + 1 < NA:
                            MI(k + 1)
                        if k == 0:
                            R_wal.wait(t, wal_base + 4)
                        MO(k)
                        STATS3(k)

                @blk.gpsimd
                def _(g):
                    barrier(g, guard)
                    for i in range(NA):
                        g.wait_ge(VCH.h, VC["x2", NCH * (i + 1) - 1])
                        dst = x2_d[:, :, i * TA:(i + 1) * TA].transpose(
                            [1, 0, 2])
                        R_x2sp.dma(g, dst, f3(x2b, 0))
                        g.wait_ge(VCH.h, VC["ln3", i])
                        dst2 = ln_d[:, :, i * TA:(i + 1) * TA].transpose(
                            [1, 0, 2])
                        R_ln3sp.dma(g, dst2, f3(ln3b, 0))
                    DMA(g, S["sp"], cc_in[0:1, :].transpose([1, 0]),
                        rmax[:, 0:1])
                    g.wait_ge(S["sp"].h, S["sp"].n)
                    S["cc"].inc(g.collective_compute(
                        "AllReduce", OP.max, replica_groups=[list(range(8))],
                        ins=[cc_in], outs=[cc_out]))
            es.close()

    with nc.sbuf_tensor("w1q_sb", [128, NCH, H], BF16) as w1q_sb:
        w1q_vc = emit_b12(w1q_sb)
        g12 = take_guard()
        hsp1_base = R_hsp.k
        emit_passB(ncc=2, col=0,
                   w1q_ap=lambda c, j: bass.AP(
                       w1q_sb, c * H + j * 128, [[NCH * H, 128], [1, 128]]),
                   first=True, guard=g12,
                   ln_wait_fn=lambda s, i: R_lnsp.wait(s, i),
                   w1q_vc=w1q_vc, w1q_sb=w1q_sb)
    gB1 = take_guard()
    x1sp_base = R_x1sp.k

    def x0w_c2(g, i):
        pass

    emit_passC(ncc=3, col=3,
               res_src_fn=lambda i: xT_r[:, :, i * TT:(i + 1) * TT],
               dst_fn=lambda i: x1_d[:, :, i * TT:(i + 1) * TT]
               .transpose([1, 0, 2]),
               x0dt=F32, outdt=F16, x0_wait_fn=x0w_c2,
               guard=gB1, hsp_base=hsp1_base)
    gC2 = take_guard()
    emit_attn(guard=gC2, x1sp_base=x1sp_base)
    gAT = take_guard()

    with nc.sbuf_tensor("w1q_sb3", [128, NCH, H], BF16) as w1q_sb3:
        with nc.Block("b_w1ql") as blk:
            @blk.sync
            def _(s):
                barrier(s, gAT)
                for j in range(HCH):
                    R_w1qsp.wait(s, j)
                    dst = _flat(w1q_sb3, j * NCH * 128, NCH * 128,
                                HCH * NCH * 128)
                    src = bass.AP(w1q_t, j * 128 * NCH * 128,
                                  [[NCH * 128, 128], [1, NCH * 128]])
                    R_w1ql.dma(s, dst, src)
        hsp3_base = R_hsp.k
        emit_passB(ncc=4, col=6,
                   w1q_ap=lambda c, j: bass.AP(
                       w1q_sb3, j * NCH * 128 + c * 128,
                       [[HCH * NCH * 128, 128], [1, 128]]),
                   first=False, guard=gAT,
                   ln_wait_fn=lambda s, i: R_ln3sp.wait(s, 2 * i + 1),
                   w1q_vc=0)
    gB3 = take_guard()

    def x0w_c4(g, i):
        R_x2sp.wait(g, 2 * i + 1)

    emit_passC(ncc=5, col=9,
               res_src_fn=lambda i: x2_d[:, :, i * TT:(i + 1) * TT]
               .transpose([1, 0, 2]),
               dst_fn=lambda i: outT_r[:, :, i * TT:(i + 1) * TT],
               x0dt=F16, outdt=F32, x0_wait_fn=x0w_c4,
               guard=gB3, hsp_base=hsp3_base)

    nc.finalize()
    st.close()
    return nc


def kernel(L0_attn_x, L0_attn_W_in, L0_attn_W_alpha, L0_attn_b_alpha,
           L0_attn_W_beta, L0_attn_b_beta, L0_attn_W_v, L0_attn_b_v,
           L0_attn_W_out, L0_ffn_w1, L0_ffn_b1, L0_ffn_w2, L0_ffn_b2):
    f32 = np.float32
    x = np.asarray(L0_attn_x, dtype=f32)
    w1T = np.ascontiguousarray(np.asarray(L0_ffn_w1, f32).T)
    w2T = np.ascontiguousarray(np.asarray(L0_ffn_w2, f32).T)
    winT = np.ascontiguousarray(np.asarray(L0_attn_W_in, f32).T)
    waT = np.ascontiguousarray(np.asarray(L0_attn_W_alpha, f32).T)
    wbT = np.ascontiguousarray(np.asarray(L0_attn_W_beta, f32).T)
    wvT = np.ascontiguousarray(np.asarray(L0_attn_W_v, f32).T)
    woT = np.ascontiguousarray(np.asarray(L0_attn_W_out, f32).T)
    b1pm = np.ascontiguousarray(np.asarray(L0_ffn_b1, f32).reshape(HCH, 128).T)
    b2pm = np.ascontiguousarray(np.asarray(L0_ffn_b2, f32).reshape(NCH, 128).T)
    bapm = np.ascontiguousarray(
        np.asarray(L0_attn_b_alpha, f32).reshape(NCH, 128).T)
    bbpm = np.ascontiguousarray(
        np.asarray(L0_attn_b_beta, f32).reshape(NCH, 128).T)
    bvpm = np.ascontiguousarray(
        np.asarray(L0_attn_b_v, f32).reshape(NCH, 128).T)
    w1flat = w1T.reshape(-1)
    w2flat = w2T.reshape(-1)
    shl = w1flat.size // 8

    nc = build_program()
    in_maps = []
    for bb in range(8):
        wshard = np.ascontiguousarray(np.concatenate(
            [w1flat[bb * shl:(bb + 1) * shl].reshape(128, 4096),
             w2flat[bb * shl:(bb + 1) * shl].reshape(128, 4096)], axis=1))
        in_maps.append({
            "xT": np.ascontiguousarray(x[bb].T),
            "w1T": w1T, "w2T": w2T, "wsh": wshard,
            "winT": winT, "waT": waT, "wbT": wbT, "wvT": wvT, "woT": woT,
            "b1p": b1pm, "b2p": b2pm,
            "bap": bapm, "bbp": bbpm, "bvp": bvpm,
        })
    res = run_bass_kernel_spmd(nc, in_maps, core_ids=list(range(8)))
    out = np.stack([np.asarray(res.results[bb]["outT"]).T for bb in range(8)])
    return out.astype(np.float32)
